# version 02
# baseline (speedup 1.0000x reference)
import math
import numpy as np
import ml_dtypes
from contextlib import ExitStack

import concourse.bass as bass
import concourse.mybir as mybir
from concourse.bass_utils import run_bass_kernel_spmd

F32 = mybir.dt.float32
BF16 = mybir.dt.bfloat16
I32 = mybir.dt.int32
AF = mybir.ActivationFunctionType
ALU = mybir.AluOpType
AX = mybir.AxisListType

B, S, D = 4, 4096, 2048
H, KVH, HD = 16, 4, 128
KVD = KVH * HD
FI = 4096
NT = S // 128
OT = NT // 2
NCH = D // 128
NCH2 = FI // 128
EPS_LN = 1e-5
EPS_RMS = 1.1920929e-07
EPS_FFN = 1e-5
SQHD = math.sqrt(HD)
MAGIC = 1.5 * 2.0 ** 23

STAGES = 5
DO_S2 = True
TERN_PART = 3
DO_TERN = True
DO_S3 = True
DO_S5 = True
DBG = False

_UID = [0]
def _nm(base):
    _UID[0] += 1
    return f"{base}_{_UID[0]}"


class _VGuard:

    _PASS = ("wait_ge", "wait_op", "sem_inc", "br", "nop", "halt",
             "dbg_callback", "breakpoint")

    def __init__(self, eng):
        self._e = eng
        self._hist = []
        self.n_drains = 0

    def __getattr__(self, name):
        e = self.__dict__["_e"]
        fn = getattr(e, name)
        if not callable(fn) or name.startswith("_"):
            return fn
        hist = self.__dict__["_hist"]
        if name == "drain":
            def g(*a, **k):
                hist.clear()
                return fn(*a, **k)
            return g
        if name in _VGuard._PASS:
            return fn

        def g(*a, **k):
            aps = [x for x in a if hasattr(x, "tensor")]
            outs, ins = set(), set()
            if aps:
                outs.add(aps[0].tensor.name)
                for x in aps[1:]:
                    ins.add(x.tensor.name)
            for kw, val in k.items():
                if hasattr(val, "tensor"):
                    if kw == "accum_out":
                        outs.add(val.tensor.name)
                    else:
                        ins.add(val.tensor.name)
            if any(ins & h for h in hist):
                e.drain()
                hist.clear()
                self.n_drains += 1
            r = fn(*a, **k)
            hist.append(outs)
            if len(hist) > 2:
                hist.pop(0)
            return r
        return g


def build_program():
    nc = bass.Bass("TRN2", target_bir_lowering=False, detect_race_conditions=False)

    def DI(name, shape, dt=F32):
        return nc.dram_tensor(name, shape, dt, kind="ExternalInput").ap()

    x_tok = DI("x_tok", [S, D])
    xT_own = DI("xT_own", [D, OT * 128])
    x_own = DI("x_own", [OT, 128, D])
    wqT = DI("wqT", [D, D])
    wkT = DI("wkT", [D, KVD])
    wvT = DI("wvT", [D, KVD])
    woT = DI("woT", [D, D])
    wupT = DI("wupT", [D, FI])
    wdnT = DI("wdnT", [FI, D])
    bq_i = DI("bq", [1, D])
    bkv_i = DI("bkv", [1, 2 * KVD])
    bo_i = DI("bo", [1, D])
    qg_i = DI("qgain", [1, H])
    cos_all = DI("cos_all", [NT, 128, 64], F32)
    sin_all = DI("sin_all", [NT, 128, 64], F32)
    cos_own = DI("cos_own", [OT, 128, 64], F32)
    sin_own = DI("sin_own", [OT, 128, 64], F32)
    h128_i = DI("h128_in", [128, 128], BF16)
    h128n_i = DI("h128n_in", [128, 128], BF16)
    identb_i = DI("identb_in", [128, 128], BF16)
    identf_i = DI("identf_in", [128, 128], F32)
    maske_i = DI("mask_e", [128, 128], BF16)
    masko_i = DI("mask_o", [128, 128], BF16)

    partial_hbm = nc.dram_tensor("partial_hbm", [OT, 128, D], F32, kind="Internal").ap()
    ghi_hbm = nc.dram_tensor("ghi_hbm", [4, 128, NCH2 * 512], BF16, kind="Internal").ap()
    glo_hbm = nc.dram_tensor("glo_hbm", [4, 128, NCH2 * 512], BF16, kind="Internal").ap()
    wupt_hbm = nc.dram_tensor("wupt_hbm", [NCH, 128, FI], BF16, kind="Internal").ap()
    wdnt_hbm = nc.dram_tensor("wdnt_hbm", [NCH2, 128, D], BF16, kind="Internal").ap()
    out_hbm = nc.dram_tensor("out", [OT, 128, D], F32, kind="ExternalOutput").ap()
    if DBG:
        dbg_partial_ap = nc.dram_tensor("dbg_partial", [OT, 128, D], F32,
                                        kind="ExternalOutput").ap()
        dbg_ghi_ap = nc.dram_tensor("dbg_ghi", [4, 128, NCH2 * 512], BF16,
                                    kind="ExternalOutput").ap()
        dbg_grids_ap = nc.dram_tensor("dbg_grids", [128, 80], F32,
                                      kind="ExternalOutput").ap()
        dbg_tern_ap = nc.dram_tensor("dbg_tern", [128, 36], F32,
                                     kind="ExternalOutput").ap()
    else:
        dbg_partial_ap = dbg_ghi_ap = dbg_grids_ap = dbg_tern_ap = None

    es = ExitStack()
    names = ["LD", "GP", "TE", "VE", "SE"]
    sem = {n: es.enter_context(nc.semaphore(n)) for n in names}
    cnt = {n: 0 for n in names}
    eng = {"LD": nc.sync, "GP": nc.gpsimd, "TE": nc.tensor, "VE": nc.vector,
           "SE": nc.scalar}
    waited = {n: {m: 0 for m in names} for n in names}

    def inc(name, instr, n=None):
        if n is None:
            n = 16 if name in ("LD", "GP") else 1
        instr.then_inc(sem[name], n)
        cnt[name] += n
        return (name, cnt[name])

    def tok(name):
        return (name, cnt[name])

    def wait(who, *tokens):
        e = eng[who]
        for t in tokens:
            if t is None:
                continue
            name, val = t
            if name == who:
                continue
            if val > waited[who][name]:
                e.wait_ge(sem[name], val)
                waited[who][name] = val

    def barrier(whos=None):
        toks = [(n, cnt[n]) for n in names]
        for w in (whos or names):
            wait(w, *toks)

    V = _VGuard(nc.vector)
    SC, T, SY, GP = nc.scalar, nc.tensor, nc.sync, nc.gpsimd

    identb = es.enter_context(nc.sbuf_tensor(_nm("identb"), [128, 128], BF16))
    identf = es.enter_context(nc.sbuf_tensor(_nm("identf"), [128, 128], F32))
    h128 = es.enter_context(nc.sbuf_tensor(_nm("h128"), [128, 128], BF16))
    h128n = es.enter_context(nc.sbuf_tensor(_nm("h128n"), [128, 128], BF16))
    onesb = es.enter_context(nc.sbuf_tensor(_nm("onesb"), [128, 128], BF16))
    onestok = es.enter_context(nc.sbuf_tensor(_nm("onestok"), [128, 128], F32))
    maskes = es.enter_context(nc.sbuf_tensor(_nm("maskes"), [128, 128], BF16))
    maskos = es.enter_context(nc.sbuf_tensor(_nm("maskos"), [128, 128], BF16))
    bgrid = es.enter_context(nc.sbuf_tensor(_nm("bgrid"), [128, H], F32))
    qgg = es.enter_context(nc.sbuf_tensor(_nm("qgg"), [128, H], F32))
    s_own = es.enter_context(nc.sbuf_tensor(_nm("s_own"), [128, OT], F32))
    r1_grid = es.enter_context(nc.sbuf_tensor(_nm("r1_grid"), [128, OT], F32))
    fs_grid = es.enter_context(nc.sbuf_tensor(_nm("fs_grid"), [128, OT], F32))
    wsu_col = es.enter_context(nc.sbuf_tensor(_nm("wsu_col"), [128, 1], F32))
    wsd_col = es.enter_context(nc.sbuf_tensor(_nm("wsd_col"), [128, 1], F32))
    bq_pad = es.enter_context(nc.sbuf_tensor(_nm("bq_pad"), [128, D], BF16))
    bkv_pad = es.enter_context(nc.sbuf_tensor(_nm("bkv_pad"), [128, 2 * KVD], BF16))
    bo_pad = es.enter_context(nc.sbuf_tensor(_nm("bo_pad"), [128, D], BF16))
    padf = es.enter_context(nc.sbuf_tensor(_nm("padf"), [128, 2], F32))
    bshadow = es.enter_context(nc.sbuf_tensor(_nm("bshadow"), [128, H], F32))
    dbgT = es.enter_context(nc.sbuf_tensor(_nm("dbgT"), [128, 36], F32))
    tinyf = es.enter_context(nc.sbuf_tensor(_nm("tinyf"), [128, 16], F32))

    with es:
        inc("LD", SY.dma_start(identb[:, :], identb_i))
        inc("LD", SY.dma_start(identf[:, :], identf_i))
        inc("LD", SY.dma_start(h128[:, :], h128_i))
        inc("LD", SY.dma_start(h128n[:, :], h128n_i))
        inc("LD", SY.dma_start(maskes[:, :], maske_i))
        inc("LD", SY.dma_start(maskos[:, :], masko_i))
        V.memset(onesb[:, :], 1.0)
        V.memset(onestok[:, :], 1.0)
        V.memset(bq_pad[:, :], 0.0)
        V.memset(bkv_pad[:, :], 0.0)
        V.memset(bo_pad[:, :], 0.0)
        V.memset(dbgT[:, :], 0.0)
        inc("VE", V.memset(padf[:, :], 0.0))

        with nc.sbuf_tensor(_nm("qg_row"), [1, H], F32) as qg_row, \
             nc.sbuf_tensor(_nm("qg_pad"), [128, 2 * H], BF16) as qg_pad, \
             nc.psum_tensor(_nm("ps_qg"), [128, 2 * H], F32) as ps_qg:
            inc("LD", SY.dma_start(qg_row[:, :], qg_i))
            wait("VE", tok("LD"))
            V.memset(qg_pad[:, :], 0.0)
            inc("VE", V.tensor_copy(qg_pad[0:1, 0:H], qg_row[:, :]))
            wait("SE", tok("VE"), tok("LD"))
            inc("SE", SC.activation(qg_pad[0:1, H:2 * H], qg_row[:, :], AF.Abs))
            wait("TE", tok("SE"), tok("VE"))
            inc("TE", T.matmul(ps_qg[:, :], onesb[:, :], qg_pad[:, :],
                               start=True, stop=True))
            wait("VE", tok("TE"))
            V.tensor_scalar(qgg[:, :], ps_qg[:, 0:H], 1.0 / SQHD, 0.0,
                            ALU.mult, ALU.add)
            V.tensor_scalar(bgrid[:, :], ps_qg[:, H:2 * H], -SQHD, 0.0,
                            ALU.mult, ALU.add)
            inc("VE", V.tensor_copy(bshadow[:, :], bgrid[:, :]))
        if DBG:
            dbg_early_ap = nc.dram_tensor("dbg_early", [128, H], F32,
                                          kind="ExternalOutput").ap()
            wait("LD", tok("VE"))
            inc("LD", SY.dma_start(dbg_early_ap, bgrid[:, :]))
        barrier()
        with nc.sbuf_tensor(_nm("brow"), [1, 2 * D + 2 * KVD], F32) as brow:
            inc("LD", SY.dma_start(brow[:, 0:D], bq_i))
            inc("LD", SY.dma_start(brow[:, D:D + 2 * KVD], bkv_i))
            inc("LD", SY.dma_start(brow[:, D + 2 * KVD:2 * D + 2 * KVD], bo_i))
            wait("VE", tok("LD"))
            V.tensor_copy(bq_pad[0:1, :], brow[:, 0:D])
            V.tensor_copy(bkv_pad[0:1, :], brow[:, D:D + 2 * KVD])
            inc("VE", V.tensor_copy(bo_pad[0:1, :],
                                    brow[:, D + 2 * KVD:2 * D + 2 * KVD]))
        barrier(["TE", "VE", "SE", "LD", "GP"])

        def ternarize(wT_ap, dst_hbm, ntiles, ncols, ws_col, use_sign):
            tes = ExitStack()
            wa = tes.enter_context(nc.sbuf_tensor(_nm("tn_wa"), [128, ncols], F32))
            wb = tes.enter_context(nc.sbuf_tensor(_nm("tn_wb"), [128, ncols], F32))
            tmpf = tes.enter_context(nc.sbuf_tensor(_nm("tn_tmp"), [128, ncols], F32))
            scrap = tes.enter_context(nc.sbuf_tensor(_nm("tn_scrap"), [128, ncols], BF16))
            tern = tes.enter_context(nc.sbuf_tensor(_nm("tn_t"), [128, ncols], BF16))
            terns = tes.enter_context(nc.sbuf_tensor(_nm("tn_t2"), [128, ncols], BF16))
            i32s = tes.enter_context(nc.sbuf_tensor(_nm("tn_i32"), [128, ncols], I32))
            absg = tes.enter_context(nc.sbuf_tensor(_nm("tn_absg"), [128, NCH2], F32))
            hcol = tes.enter_context(nc.sbuf_tensor(_nm("tn_h"), [128, 1], F32))
            nhcol = tes.enter_context(nc.sbuf_tensor(_nm("tn_nh"), [128, 1], F32))
            ps_w = tes.enter_context(nc.psum_tensor(_nm("tn_ps"), [128, 256], F32))
            ps_wb = tes.enter_context(nc.psum_tensor(_nm("tn_psb"), [128, 256], BF16))
            hilo = tes.enter_context(nc.sbuf_tensor(_nm("tn_hilo"), [128, 8], BF16))
            absgb = tes.enter_context(nc.sbuf_tensor(_nm("tn_absgb"), [128, 256], BF16))
            hcl = tes.enter_context(nc.sbuf_tensor(_nm("tn_hcl"), [128, 32], F32))
            with tes:
                bufs, freed = [wa, wb], [None, None]
                for r in range(ntiles):
                    bi = r % 2
                    wait("LD", freed[bi])
                    lt = inc("LD", SY.dma_start(bufs[bi][:, :],
                                                wT_ap[r * 128:(r + 1) * 128, :]))
                    wait("SE", lt)
                    if TERN_PART >= 1:
                        freed[bi] = inc("SE", SC.activation(
                            scrap[:, :], bufs[bi][:, :], AF.Abs,
                            accum_out=absg[:, r:r + 1]))
                    else:
                        freed[bi] = inc("SE", SC.copy(scrap[:, 0:128],
                                                      bufs[bi][:, 0:128]))
                wait("VE", tok("SE"))
                inc("VE", V.tensor_copy(dbgT[:, 0:32] if ntiles > NCH else dbgT[:, 0:16],
                                        absg[:, 0:ntiles] if ntiles > NCH else absg[:, 0:16]))
                if TERN_PART < 2:
                    return
                V.memset(absgb[:, :], 0.0)
                V.tensor_copy(absgb[:, 0:ntiles], absg[:, 0:ntiles])
                V.tensor_copy(hcl[:, 0:ntiles], absgb[:, 0:ntiles])
                V.tensor_tensor(hcl[:, 0:ntiles], absg[:, 0:ntiles],
                                hcl[:, 0:ntiles], ALU.subtract)
                inc("VE", V.tensor_copy(absgb[:, 128:128 + ntiles],
                                        hcl[:, 0:ntiles]))
                wait("TE", tok("VE"))
                inc("TE", T.matmul(ps_w[:, 0:256], onesb[:, :], absgb[:, 0:256],
                                   start=True, stop=True))
                wait("VE", tok("TE"))
                V.reduce_sum(hcl[:, 0:1], ps_w[:, 0:256], axis=AX.X)
                V.drain()
                inc("VE", V.tensor_scalar(ws_col[:, :], hcl[:, 0:1],
                                          1.0 / (ntiles * 128 * ncols), EPS_FFN,
                                          ALU.mult, ALU.max))
                inc("VE", V.tensor_copy(dbgT[:, 32:33], ws_col[:, :]))
                if TERN_PART < 3:
                    return
                if use_sign:
                    V.tensor_scalar(hcol[:, :], ws_col[:, :], -0.5, 0.0,
                                    ALU.mult, ALU.add)
                    inc("VE", V.tensor_scalar(nhcol[:, :], ws_col[:, :], 0.5, 0.0,
                                              ALU.mult, ALU.add))
                else:
                    inc("VE", V.reciprocal(hcol[:, :], ws_col[:, :]))
                freed = [None, None]
                st_tok = None
                for r in range(ntiles):
                    bi = r % 2
                    wait("LD", freed[bi])
                    lt = inc("LD", SY.dma_start(bufs[bi][:, :],
                                                wT_ap[r * 128:(r + 1) * 128, :]))
                    if use_sign:
                        wait("SE", lt, tok("VE"))
                        SC.sign(tern[:, :], bufs[bi][:, :], bias=hcol[:, :])
                        freed[bi] = inc("SE", SC.sign(scrap[:, :], bufs[bi][:, :],
                                                      bias=nhcol[:, :]))
                        wait("VE", freed[bi], st_tok)
                        vt = inc("VE", V.tensor_tensor(terns[:, :], tern[:, :],
                                                       scrap[:, :], ALU.add))
                    else:
                        wait("VE", lt, st_tok)
                        V.tensor_scalar_mul(tmpf[:, :], bufs[bi][:, :], hcol[:, :])
                        freed[bi] = tok("VE")
                        V.tensor_copy(i32s[:, :], tmpf[:, :])
                        V.tensor_copy(tern[:, :], i32s[:, :])
                        vt = inc("VE", V.tensor_scalar(terns[:, :], tern[:, :],
                                                       1.0, -1.0, ALU.min, ALU.max))
                        freed[bi] = vt
                    wait("GP", vt)
                    st_tok = inc("GP", GP.dma_start(dst_hbm[r, :, :], terns[:, :]))
                barrier()

        if STAGES >= 5 and DO_TERN:
            ternarize(wupT, wupt_hbm, NCH, FI, wsu_col, use_sign=True)
            ternarize(wdnT, wdnt_hbm, NCH2, D, wsd_col, use_sign=True)
        barrier()

        s23 = ExitStack()
        KT_sb = s23.enter_context(nc.sbuf_tensor(_nm("KT_sb"), [128, KVH * S], BF16))
        Vaug = s23.enter_context(nc.sbuf_tensor(_nm("Vaug"), [128, KVH * NT * 130], BF16))
        xc = s23.enter_context(nc.sbuf_tensor(_nm("xc"), [128, D], BF16))
        diag = s23.enter_context(nc.sbuf_tensor(_nm("diag"), [128, 128], BF16))
        bn6 = s23.enter_context(nc.sbuf_tensor(_nm("bn6"), [128, 24], F32))
        mv = s23.enter_context(nc.sbuf_tensor(_nm("mv"), [128, 2], F32))
        rta = s23.enter_context(nc.sbuf_tensor(_nm("rta"), [128, KVH * 64], BF16))
        rtb = s23.enter_context(nc.sbuf_tensor(_nm("rtb"), [128, KVH * 64], BF16))
        Vaug4 = Vaug[:, :].rearrange("p (kv t c) -> p kv t c", kv=KVH, t=NT)
        KT4 = KT_sb[:, :].rearrange("p (kv s) -> p kv s", kv=KVH)

        with s23:
            s2 = ExitStack()
            cosb = s2.enter_context(nc.sbuf_tensor(_nm("cosb"), [128, NT * 64], BF16))
            sinb = s2.enter_context(nc.sbuf_tensor(_nm("sinb"), [128, NT * 64], BF16))
            ctf = s2.enter_context(nc.sbuf_tensor(_nm("ctf"), [128, NT * 64], F32))
            wk_sb = s2.enter_context(nc.sbuf_tensor(_nm("wk_sb"), [128, NCH * KVD], BF16))
            wv_sb = s2.enter_context(nc.sbuf_tensor(_nm("wv_sb"), [128, NCH * KVD], BF16))
            xld = [s2.enter_context(nc.sbuf_tensor(_nm(f"x_{i}"), [128, D], F32))
                   for i in range(2)]
            xn_sb = [s2.enter_context(nc.sbuf_tensor(_nm(f"xn_{i}"), [128, D], BF16))
                     for i in range(2)]
            kt_f = s2.enter_context(nc.sbuf_tensor(_nm("kt_f"), [128, KVD], F32))
            sqv = s2.enter_context(nc.sbuf_tensor(_nm("sqv"), [128, KVD], F32))
            ksum = s2.enter_context(nc.sbuf_tensor(_nm("ksum"), [128, KVH], F32))
            krs = s2.enter_context(nc.sbuf_tensor(_nm("krs"), [128, KVH], F32))
            kt_b = s2.enter_context(nc.sbuf_tensor(_nm("kt_b"), [128, KVD], BF16))
            krt = s2.enter_context(nc.sbuf_tensor(_nm("krt"), [128, KVD], BF16))
            ps_ln = s2.enter_context(nc.psum_tensor(_nm("ps_ln"), [128, D], F32))
            ps_k = s2.enter_context(nc.psum_tensor(_nm("ps_k"), [128, KVD], F32))
            ps_v = s2.enter_context(nc.psum_tensor(_nm("ps_v"), [128, KVD], F32))
            ps_tr = s2.enter_context(nc.psum_tensor(_nm("ps_tr"), [128, KVD], BF16))
            with s2:
              if DO_S2:
                  l1 = inc("LD", SY.dma_start(
                      ctf[:, :].rearrange("p (t f) -> p t f", t=NT),
                      cos_all.rearrange("t p f -> p t f")))
                  wait("VE", l1)
                  c1 = inc("VE", V.tensor_copy(cosb[:, :], ctf[:, :]))
                  wait("LD", c1)
                  l2 = inc("LD", SY.dma_start(
                      ctf[:, :].rearrange("p (t f) -> p t f", t=NT),
                      sin_all.rearrange("t p f -> p t f")))
                  wait("VE", l2)
                  cos_tok = inc("VE", V.tensor_copy(sinb[:, :], ctf[:, :]))
                  inc("GP", GP.dma_start(
                      wk_sb[:, :].rearrange("p (c e) -> p c e", c=NCH),
                      wkT.rearrange("(c p) e -> p c e", p=128)))
                  wload = inc("GP", GP.dma_start(
                      wv_sb[:, :].rearrange("p (c e) -> p c e", c=NCH),
                      wvT.rearrange("(c p) e -> p c e", p=128)))
                  inc("VE", V.memset(Vaug4[:, :, :, 128:129], 1.0))

                  ln_evac = None
                  xn_free = [None, None]
                  x_free = [None, None]
                  xc_free = None
                  kv_evac = None
                  prep_done = None
                  tr_done = None
                  for t in range(NT):
                      bi = t % 2
                      wait("LD", x_free[bi])
                      lx = inc("LD", SY.dma_start(xld[bi][:, :],
                                                  x_tok[t * 128:(t + 1) * 128, :]))
                      wait("VE", lx)
                      for gi in range(4):
                          V.bn_stats(bn6[:, gi * 6:(gi + 1) * 6],
                                     xld[bi][:, gi * 512:(gi + 1) * 512])
                      V.bn_aggr(mv[:, :], bn6[:, :])
                      V.tensor_scalar(tinyf[:, 0:1], mv[:, 1:2], 1.0, EPS_LN,
                                      ALU.mult, ALU.add)
                      vt = inc("VE", V.reciprocal(tinyf[:, 1:2], tinyf[:, 0:1]))
                      wait("SE", vt)
                      st = inc("SE", SC.activation(tinyf[:, 2:3], tinyf[:, 1:2], AF.Sqrt))
                      wait("VE", st, xc_free)
                      V.tensor_scalar(xc[:, :], xld[bi][:, :], mv[:, 0:1], None,
                                      ALU.subtract)
                      x_free[bi] = cd = inc("VE", V.tensor_scalar_mul(
                          diag[:, :], identb[:, :], tinyf[:, 2:3]))
                      wait("TE", cd, ln_evac)
                      for c in range(NCH):
                          mm = T.matmul(ps_ln[:, c * 128:(c + 1) * 128],
                                        xc[:, c * 128:(c + 1) * 128], diag[:, :],
                                        start=True, stop=True)
                      xc_free = dg = inc("TE", mm)
                      wait("SE", dg, xn_free[bi])
                      ln_evac = inc("SE", SC.copy(xn_sb[bi][:, :], ps_ln[:, :]))
                      if t == 0:
                          wait("TE", wload)
                      wait("TE", ln_evac, kv_evac)
                      for c in range(NCH):
                          T.matmul(ps_k[:, :], xn_sb[bi][:, c * 128:(c + 1) * 128],
                                   wk_sb[:, c * KVD:(c + 1) * KVD],
                                   start=(c == 0), stop=False)
                          T.matmul(ps_v[:, :], xn_sb[bi][:, c * 128:(c + 1) * 128],
                                   wv_sb[:, c * KVD:(c + 1) * KVD],
                                   start=(c == 0), stop=False)
                      T.matmul(ps_k[:, :], onesb[:, :], bkv_pad[:, 0:KVD],
                               start=False, stop=True)
                      xn_free[bi] = kv_done = inc("TE", T.matmul(
                          ps_v[:, :], onesb[:, :], bkv_pad[:, KVD:2 * KVD],
                          start=False, stop=True))
                      wait("SE", kv_done, prep_done)
                      inc("SE", SC.copy(
                          Vaug4[:, :, t, 0:128],
                          ps_v[:, :].rearrange("p (kv c) -> p kv c", kv=KVH)))
                      kv_evac = se_k = inc("SE", SC.copy(kt_f[:, :], ps_k[:, :]))
                      wait("VE", se_k, cos_tok)
                      V.tensor_mul(sqv[:, :], kt_f[:, :], kt_f[:, :])
                      V.reduce_sum(ksum[:, :],
                                   sqv[:, :].rearrange("p (kv c) -> p kv c", kv=KVH),
                                   axis=AX.X)
                      V.tensor_scalar(tinyf[:, 4:8], ksum[:, :], 1.0 / HD, EPS_RMS,
                                      ALU.mult, ALU.add)
                      vr = inc("VE", V.reciprocal(tinyf[:, 8:12], tinyf[:, 4:8]))
                      wait("SE", vr)
                      sq = inc("SE", SC.activation(krs[:, :], tinyf[:, 8:12], AF.Sqrt))
                      wait("VE", sq)
                      kb4 = kt_b[:, :].rearrange("p (kv c) -> p kv c", kv=KVH)
                      V.tensor_tensor(
                          kb4, kt_f[:, :].rearrange("p (kv c) -> p kv c", kv=KVH),
                          krs[:, :].rearrange("p (kv o) -> p kv o", o=1).broadcast_to([128, KVH, HD]),
                          ALU.mult)
                      kr4 = krt[:, :].rearrange("p (kv c) -> p kv c", kv=KVH)
                      cosr = cosb[:, t * 64:(t + 1) * 64].rearrange(
                          "p (o f) -> p o f", o=1).broadcast_to([128, KVH, 64])
                      sinr = sinb[:, t * 64:(t + 1) * 64].rearrange(
                          "p (o f) -> p o f", o=1).broadcast_to([128, KVH, 64])
                      rta4 = rta[:, :].rearrange("p (kv f) -> p kv f", kv=KVH)
                      rtb4 = rtb[:, :].rearrange("p (kv f) -> p kv f", kv=KVH)
                      V.tensor_tensor(rta4, kb4[:, :, 0:64], cosr, ALU.mult)
                      V.tensor_tensor(rtb4, kb4[:, :, 64:128], sinr, ALU.mult)
                      V.tensor_tensor(kr4[:, :, 0:64], rta4, rtb4, ALU.add)
                      V.tensor_tensor(rta4, kb4[:, :, 64:128], cosr, ALU.mult)
                      V.tensor_tensor(rtb4, kb4[:, :, 0:64], sinr, ALU.mult)
                      prep_done = inc("VE", V.tensor_tensor(
                          kr4[:, :, 64:128], rta4, rtb4, ALU.subtract))
                      wait("TE", prep_done, tr_done)
                      for kv in range(KVH):
                          tt = T.transpose(ps_tr[:, kv * 128:(kv + 1) * 128],
                                           krt[:, kv * 128:(kv + 1) * 128], identb[:, :])
                      ktr = inc("TE", tt)
                      wait("VE", ktr)
                      tr_done = inc("VE", V.tensor_copy(
                          KT4[:, :, t * 128:(t + 1) * 128],
                          ps_tr[:, :].rearrange("p (kv c) -> p kv c", kv=KVH)))
            barrier()

            if STAGES >= 3 and DO_S3:
                s3 = ExitStack()
                coso = s3.enter_context(nc.sbuf_tensor(_nm("coso"), [128, OT * 64], BF16))
                sino = s3.enter_context(nc.sbuf_tensor(_nm("sino"), [128, OT * 64], BF16))
                ctf3 = s3.enter_context(nc.sbuf_tensor(_nm("ctf3"), [128, OT * 64], F32))
                xoa = s3.enter_context(nc.sbuf_tensor(_nm("xoa"), [128, D], F32))
                xnq = s3.enter_context(nc.sbuf_tensor(_nm("xnq"), [128, 4 * D], BF16))
                QT_sb = s3.enter_context(nc.sbuf_tensor(_nm("QT_sb"), [128, H * 512], BF16))
                OT_sb = s3.enter_context(nc.sbuf_tensor(_nm("OT_sb"), [128, H * 512], BF16))
                wbuf = s3.enter_context(nc.sbuf_tensor(_nm("wbuf"), [128, NCH * 1024], BF16))
                qt_f = s3.enter_context(nc.sbuf_tensor(_nm("qt_f"), [128, 512], F32))
                qsq = s3.enter_context(nc.sbuf_tensor(_nm("qsq"), [128, 512], F32))
                qt_b = s3.enter_context(nc.sbuf_tensor(_nm("qt_b"), [128, 512], BF16))
                qrt = s3.enter_context(nc.sbuf_tensor(_nm("qrt"), [128, 512], BF16))
                o_tok = s3.enter_context(nc.sbuf_tensor(_nm("o_tok"), [128, 128], BF16))
                rcol = s3.enter_context(nc.sbuf_tensor(_nm("rcol"), [128, 1], F32))
                PT = [s3.enter_context(nc.sbuf_tensor(_nm(f"PT{i}"), [128, 512], BF16))
                      for i in range(2)]
                xres = s3.enter_context(nc.sbuf_tensor(_nm("xres"), [128, 1024], F32))
                outp = s3.enter_context(nc.sbuf_tensor(_nm("outp"), [128, 1024], F32))
                QT16 = QT_sb[:, :].rearrange("p (h s) -> p h s", h=H)
                with s3:
                    l1 = inc("LD", SY.dma_start(
                        ctf3[:, :].rearrange("p (t f) -> p t f", t=OT),
                        cos_own.rearrange("t p f -> p t f")))
                    wait("VE", l1)
                    c1 = inc("VE", V.tensor_copy(coso[:, :], ctf3[:, :]))
                    wait("LD", c1)
                    l2 = inc("LD", SY.dma_start(
                        ctf3[:, :].rearrange("p (t f) -> p t f", t=OT),
                        sin_own.rearrange("t p f -> p t f")))
                    wait("VE", l2)
                    coso_tok = inc("VE", V.tensor_copy(sino[:, :], ctf3[:, :]))
                    for qc in range(4):
                        _s3_qc(qc, nc, locals())
            barrier()
        barrier()

        if STAGES >= 5 and DO_S5:
            _s5_ffn(nc, locals())

        if DBG:
            with nc.sbuf_tensor(_nm("dbg_g"), [128, 80], F32) as dbg_g:
                barrier()
                V.memset(dbg_g[:, :], 0.0)
                V.tensor_copy(dbg_g[:, 0:OT], s_own[:, :])
                V.tensor_copy(dbg_g[:, 16:16 + OT], r1_grid[:, :])
                V.tensor_copy(dbg_g[:, 32:32 + OT], fs_grid[:, :])
                V.tensor_copy(dbg_g[:, 48:49], wsu_col[:, :])
                V.tensor_copy(dbg_g[:, 49:50], wsd_col[:, :])
                V.tensor_copy(dbg_g[:, 50:50 + H], bgrid[:, :])
                V.tensor_copy(dbg_g[:, 66:66 + 14], bshadow[:, 0:14])
                dbg_g2 = None
                inc("VE", V.engine_nop())
                wait("LD", tok("VE"))
                inc("LD", SY.dma_start(dbg_grids_ap, dbg_g[:, :]))
                inc("LD", SY.dma_start(dbg_tern_ap, dbgT[:, :]))
        barrier(["LD", "GP"])

    return nc


def _s3_qc(qc, nc, env):
    (inc, tok, wait, V, SC, T, SY, GP) = (env["inc"], env["tok"], env["wait"],
                                          env["V"], env["SC"], env["T"],
                                          env["SY"], env["GP"])
    identb, identf, onesb = env["identb"], env["identf"], env["onesb"]
    maskes, maskos = env["maskes"], env["maskos"]
    bgrid, qgg, s_own = env["bgrid"], env["qgg"], env["s_own"]
    bq_pad, bo_pad, tinyf = env["bq_pad"], env["bo_pad"], env["tinyf"]
    x_own, wqT, woT = env["x_own"], env["wqT"], env["woT"]
    partial_hbm = env["partial_hbm"]
    KT4, Vaug4 = env["KT4"], env["Vaug4"]
    xc, diag, bn6, mv, rta, rtb = (env["xc"], env["diag"], env["bn6"], env["mv"],
                                   env["rta"], env["rtb"])
    coso, sino, xoa, xnq = env["coso"], env["sino"], env["xoa"], env["xnq"]
    QT_sb, OT_sb, wbuf = env["QT_sb"], env["OT_sb"], env["wbuf"]
    qt_f, qsq, qt_b, qrt = env["qt_f"], env["qsq"], env["qt_b"], env["qrt"]
    o_tok, rcol, PT = env["o_tok"], env["rcol"], env["PT"]
    xres, outp = env["xres"], env["outp"]
    QT16, coso_tok = env["QT16"], env["coso_tok"]

    pa = ExitStack()
    ps_xn = pa.enter_context(nc.psum_tensor(_nm("ps_xn"), [128, D], F32))
    ps_q = [pa.enter_context(nc.psum_tensor(_nm(f"ps_q{i}"), [128, 512], F32))
            for i in range(2)]
    ps_qtr = pa.enter_context(nc.psum_tensor(_nm("ps_qtr"), [128, 512], BF16))
    with pa:
        xn_ev = None
        xoa_free = tok("VE")
        for lt in range(4):
            i = qc * 4 + lt
            wait("LD", xoa_free)
            lx = inc("LD", SY.dma_start(xoa[:, :], x_own[i, :, :]))
            wait("VE", lx, tok("TE"))
            for gi in range(4):
                V.bn_stats(bn6[:, gi * 6:(gi + 1) * 6],
                           xoa[:, gi * 512:(gi + 1) * 512])
            V.bn_aggr(mv[:, :], bn6[:, :])
            V.drain()
            V.tensor_scalar(tinyf[:, 0:1], mv[:, 1:2], 1.0, EPS_LN,
                            ALU.mult, ALU.add)
            vt = inc("VE", V.reciprocal(tinyf[:, 1:2], tinyf[:, 0:1]))
            wait("SE", vt)
            st = inc("SE", SC.activation(s_own[:, i:i + 1], tinyf[:, 1:2], AF.Sqrt))
            wait("VE", st)
            V.tensor_scalar(xc[:, :], xoa[:, :], mv[:, 0:1], None, ALU.subtract)
            xoa_free = cd = inc("VE", V.tensor_scalar_mul(
                diag[:, :], identb[:, :], s_own[:, i:i + 1]))
            wait("TE", cd, xn_ev)
            for c in range(NCH):
                mm = T.matmul(ps_xn[:, c * 128:(c + 1) * 128],
                              xc[:, c * 128:(c + 1) * 128], diag[:, :],
                              start=True, stop=True)
            dg = inc("TE", mm)
            wait("SE", dg)
            xn_ev = inc("SE", SC.copy(xnq[:, lt * D:(lt + 1) * D], ps_xn[:, :]))
        for kvh in range(KVH):
            wait("GP", tok("TE"))
            wl = inc("GP", GP.dma_start(
                wbuf[:, 0:NCH * 512].rearrange("p (c e) -> p c e", c=NCH),
                wqT.rearrange("(c p) e -> p c e", p=128)[:, :, kvh * 512:(kvh + 1) * 512]))
            qtr_ev = None
            for lt in range(4):
                i = qc * 4 + lt
                pq = ps_q[lt % 2]
                wait("TE", wl, xn_ev, tok("SE"))
                for c in range(NCH):
                    T.matmul(pq[:, :],
                             xnq[:, lt * D + c * 128:lt * D + (c + 1) * 128],
                             wbuf[:, c * 512:(c + 1) * 512],
                             start=(c == 0), stop=False)
                qm = inc("TE", T.matmul(pq[:, :], onesb[:, :],
                                        bq_pad[:, kvh * 512:(kvh + 1) * 512],
                                        start=False, stop=True))
                wait("SE", qm, tok("VE"))
                qe = inc("SE", SC.copy(qt_f[:, :], pq[:, :]))
                wait("VE", qe, coso_tok)
                V.tensor_mul(qsq[:, :], qt_f[:, :], qt_f[:, :])
                V.reduce_sum(tinyf[:, 4:8],
                             qsq[:, :].rearrange("p (h c) -> p h c", h=4),
                             axis=AX.X)
                V.drain()
                V.tensor_scalar(tinyf[:, 4:8], tinyf[:, 4:8], 1.0 / HD, EPS_RMS,
                                ALU.mult, ALU.add)
                vr = inc("VE", V.reciprocal(tinyf[:, 8:12], tinyf[:, 4:8]))
                wait("SE", vr)
                sq = inc("SE", SC.activation(tinyf[:, 12:16], tinyf[:, 8:12], AF.Sqrt))
                wait("VE", sq)
                V.tensor_tensor(tinyf[:, 4:8], tinyf[:, 12:16],
                                qgg[:, kvh * 4:(kvh + 1) * 4], ALU.mult)
                qb4 = qt_b[:, :].rearrange("p (h c) -> p h c", h=4)
                V.tensor_tensor(
                    qb4, qt_f[:, :].rearrange("p (h c) -> p h c", h=4),
                    tinyf[:, 4:8].rearrange("p (h o) -> p h o", o=1).broadcast_to([128, 4, HD]),
                    ALU.mult)
                qr4 = qrt[:, :].rearrange("p (h c) -> p h c", h=4)
                cosr = coso[:, i * 64:(i + 1) * 64].rearrange(
                    "p (o f) -> p o f", o=1).broadcast_to([128, 4, 64])
                sinr = sino[:, i * 64:(i + 1) * 64].rearrange(
                    "p (o f) -> p o f", o=1).broadcast_to([128, 4, 64])
                rta4 = rta[:, :].rearrange("p (h f) -> p h f", h=4)
                rtb4 = rtb[:, :].rearrange("p (h f) -> p h f", h=4)
                V.tensor_tensor(rta4, qb4[:, :, 0:64], cosr, ALU.mult)
                V.tensor_tensor(rtb4, qb4[:, :, 64:128], sinr, ALU.mult)
                V.tensor_tensor(qr4[:, :, 0:64], rta4, rtb4, ALU.add)
                V.tensor_tensor(rta4, qb4[:, :, 64:128], cosr, ALU.mult)
                V.tensor_tensor(rtb4, qb4[:, :, 0:64], sinr, ALU.mult)
                rp = inc("VE", V.tensor_tensor(qr4[:, :, 64:128], rta4, rtb4,
                                               ALU.subtract))
                wait("TE", rp, qtr_ev)
                for hl in range(4):
                    tq = T.transpose(ps_qtr[:, hl * 128:(hl + 1) * 128],
                                     qrt[:, hl * 128:(hl + 1) * 128], identb[:, :])
                tq = inc("TE", tq)
                wait("VE", tq)
                qtr_ev = inc("VE", V.tensor_copy(
                    QT16[:, kvh * 4:(kvh + 1) * 4, lt * 128:(lt + 1) * 128],
                    ps_qtr[:, :].rearrange("p (h c) -> p h c", h=4)))

    pb = ExitStack()
    ps_st = [pb.enter_context(nc.psum_tensor(_nm(f"ps_st{i}"), [128, 512], F32))
             for i in range(2)]
    ps_o = [pb.enter_context(nc.psum_tensor(_nm(f"ps_o{i}"), [128, 130], F32))
            for i in range(4)]
    ps_otr = pb.enter_context(nc.psum_tensor(_nm("ps_otr"), [128, 128], BF16))
    with pb:
        wait("TE", tok("VE"), tok("SE"))
        ktmax = 8 * qc + 8
        for kvh in range(KVH):
            for hl in range(4):
                h = kvh * 4 + hl
                pt_free = [tok("TE")] * 2
                st_free = [tok("SE")] * 2
                for kt in range(ktmax):
                    db = kt % 2
                    ist = max(kt // 2 - 4 * qc, 0)
                    off = ist * 128
                    ib = kt // 2 - 4 * qc
                    wait("TE", st_free[db])
                    mmq = T.matmul(ps_st[db][:, off:512],
                                   KT4[:, kvh, kt * 128:(kt + 1) * 128],
                                   QT_sb[:, h * 512 + off:(h + 1) * 512],
                                   start=True, stop=(ib < 0))
                    if ib >= 0:
                        mk = maskes if kt % 2 == 0 else maskos
                        mmq = T.matmul(ps_st[db][:, off:off + 128],
                                       identb[:, :], mk[:, :],
                                       start=False, stop=True)
                    qk = inc("TE", mmq)
                    wait("SE", qk, pt_free[db])
                    ex = inc("SE", SC.activation(PT[db][:, off:512],
                                                 ps_st[db][:, off:512],
                                                 AF.Exp, bias=bgrid[:, h:h + 1],
                                                 scale=1.0))
                    st_free[db] = ex
                    wait("TE", ex)
                    for ii in range(ist, 4):
                        mm = T.matmul(ps_o[ii][:, 0:129],
                                      PT[db][:, ii * 128:(ii + 1) * 128],
                                      Vaug4[:, kvh, kt, 0:129],
                                      start=(kt == 0),
                                      stop=(kt == 8 * qc + 2 * ii + 1))
                    pv = inc("TE", mm)
                    pt_free[db] = pv
                ot_prev = None
                for ii in range(4):
                    wait("VE", tok("TE"))
                    V.reciprocal(rcol[:, :], ps_o[ii][:, 128:129])
                    ev = inc("VE", V.tensor_scalar_mul(o_tok[:, :],
                                                       ps_o[ii][:, 0:128],
                                                       rcol[:, :]))
                    wait("TE", ev, ot_prev)
                    tr = inc("TE", T.transpose(ps_otr[:, :], o_tok[:, :],
                                               identb[:, :]))
                    wait("SE", tr)
                    ot_prev = inc("SE", SC.copy(
                        OT_sb[:, h * 512 + ii * 128:h * 512 + (ii + 1) * 128],
                        ps_otr[:, :]))
                wait("TE", ot_prev)

    pd = ExitStack()
    ps_wo = [pd.enter_context(nc.psum_tensor(_nm(f"ps_wo{i}"), [128, 512], F32))
             for i in range(2)]
    with pd:
        store_tok = None
        for half in range(2):
            wait("GP", tok("TE"))
            wl = inc("GP", GP.dma_start(
                wbuf[:, :].rearrange("p (c e) -> p c e", c=NCH),
                woT.rearrange("(c p) e -> p c e", p=128)[
                    :, :, half * 1024:(half + 1) * 1024]))
            for lt in range(4):
                i = qc * 4 + lt
                wait("LD", tok("VE"))
                lr = inc("LD", SY.dma_start(
                    xres[:, :], x_own[i, :, half * 1024:(half + 1) * 1024]))
                for sl in range(2):
                    pw = ps_wo[sl]
                    ec = half * 1024 + sl * 512
                    wait("TE", wl, tok("VE"))
                    for c in range(NCH):
                        T.matmul(pw[:, :],
                                 OT_sb[:, c * 512 + lt * 128:c * 512 + (lt + 1) * 128],
                                 wbuf[:, c * 1024 + sl * 512:c * 1024 + (sl + 1) * 512],
                                 start=(c == 0), stop=False)
                    wm = inc("TE", T.matmul(pw[:, :], onesb[:, :],
                                            bo_pad[:, ec:ec + 512],
                                            start=False, stop=True))
                    wait("VE", wm, lr, store_tok)
                    vt = inc("VE", V.tensor_tensor(
                        outp[:, sl * 512:(sl + 1) * 512], pw[:, :],
                        xres[:, sl * 512:(sl + 1) * 512], ALU.add))
                wait("LD", vt)
                store_tok = inc("LD", SY.dma_start(
                    partial_hbm[i, :, half * 1024:(half + 1) * 1024], outp[:, :]))
                if env["dbg_partial_ap"] is not None:
                    store_tok = inc("LD", SY.dma_start(
                        env["dbg_partial_ap"][i, :, half * 1024:(half + 1) * 1024],
                        outp[:, :]))


def _s5_ffn(nc, env):
    (inc, tok, wait, barrier) = (env["inc"], env["tok"], env["wait"], env["barrier"])
    V, SC, T, SY, GP = env["V"], env["SC"], env["T"], env["SY"], env["GP"]
    identf, h128, h128n, onestok = (env["identf"], env["h128"], env["h128n"],
                                    env["onestok"])
    identb, onesb = env["identb"], env["onesb"]
    s_own, r1_grid, fs_grid = env["s_own"], env["r1_grid"], env["fs_grid"]
    wsu_col, wsd_col, tinyf = env["wsu_col"], env["wsd_col"], env["tinyf"]
    xT_own = env["xT_own"]
    partial_hbm, out_hbm = env["partial_hbm"], env["out_hbm"]
    ghi_hbm, glo_hbm = env["ghi_hbm"], env["glo_hbm"]
    wupt_hbm, wdnt_hbm = env["wupt_hbm"], env["wdnt_hbm"]

    H4 = np.array([[1, 1, 1, 1], [1, -1, 1, -1], [1, 1, -1, -1], [1, -1, -1, 1]])

    sa = ExitStack()
    xsl = sa.enter_context(nc.sbuf_tensor(_nm("xsl"), [128, NCH * 256], F32))
    xhi = sa.enter_context(nc.sbuf_tensor(_nm("xhi"), [128, NCH * 256], BF16))
    xlo = sa.enter_context(nc.sbuf_tensor(_nm("xlo"), [128, NCH * 256], BF16))
    xfA = sa.enter_context(nc.sbuf_tensor(_nm("xfA"), [128, NCH * 256], F32))
    xfB = sa.enter_context(nc.sbuf_tensor(_nm("xfB"), [128, NCH * 256], F32))
    i32a = sa.enter_context(nc.sbuf_tensor(_nm("i32a"), [128, NCH * 256], I32))
    xq1 = sa.enter_context(nc.sbuf_tensor(_nm("xq1"), [128, NCH * 512], BF16))
    mm1 = sa.enter_context(nc.sbuf_tensor(_nm("mm1"), [128, 256], F32))
    mmh = sa.enter_context(nc.sbuf_tensor(_nm("mmh"), [128, 256], BF16))
    mml = sa.enter_context(nc.sbuf_tensor(_nm("mml"), [128, 256], BF16))
    scf = sa.enter_context(nc.sbuf_tensor(_nm("scf"), [128, 256], F32))
    tinyb = sa.enter_context(nc.sbuf_tensor(_nm("tinyb"), [128, 2], BF16))
    bth = sa.enter_context(nc.sbuf_tensor(_nm("bth"), [128, 256], BF16))
    btl = sa.enter_context(nc.sbuf_tensor(_nm("btl"), [128, 256], BF16))
    wup_b = [sa.enter_context(nc.sbuf_tensor(_nm(f"wup_{i}"), [128, NCH * 128], BF16))
             for i in range(2)]
    grel = sa.enter_context(nc.sbuf_tensor(_nm("grel"), [128, 512], F32))
    g2 = sa.enter_context(nc.sbuf_tensor(_nm("g2"), [128, 512], F32))
    ghi_sb = sa.enter_context(nc.sbuf_tensor(_nm("ghi_sb"), [128, NCH2 * 512], BF16))
    glo_sb = sa.enter_context(nc.sbuf_tensor(_nm("glo_sb"), [128, NCH2 * 512], BF16))
    ps_f1 = [sa.enter_context(nc.psum_tensor(_nm(f"ps_f1{i}"), [128, 256], F32))
             for i in range(3)]
    ps_up = [sa.enter_context(nc.psum_tensor(_nm(f"ps_up{i}"), [128, 512], F32))
             for i in range(2)]
    ps_txb = sa.enter_context(nc.psum_tensor(_nm("ps_txb"), [128, 512], BF16))
    with sa:
        wt_tok = None
        g_store = None
        for grp in range(4):
            for sl2 in range(2):
                s5 = grp * 2 + sl2
                wait("LD", tok("VE"))
                lx = inc("LD", SY.dma_start(
                    xsl[:, :].rearrange("p (c t) -> p c t", c=NCH),
                    xT_own.rearrange("(c p) t -> p c t", p=128)[
                        :, :, s5 * 256:(s5 + 1) * 256]))
                wait("VE", lx, tok("TE"))
                V.tensor_copy(xhi[:, :], xsl[:, :])
                hl = inc("VE", V.tensor_tensor(xlo[:, :], xsl[:, :], xhi[:, :],
                                               ALU.subtract))
                wait("TE", hl)
                f1_free = [None, None, None]
                for chi in range(4):
                    for co in range(4):
                        pidx = (chi * 4 + co) % 3
                        ps = ps_f1[pidx]
                        wait("TE", f1_free[pidx])
                        order = sorted(range(4), key=lambda ci: -H4[co][ci])
                        first = True
                        for ci in order:
                            lhs = h128 if H4[co][ci] > 0 else h128n
                            cin = chi * 4 + ci
                            T.matmul(ps[:, :], lhs[:, :],
                                     xhi[:, cin * 256:(cin + 1) * 256],
                                     start=first, stop=False)
                            first = False
                            mm = T.matmul(ps[:, :], lhs[:, :],
                                          xlo[:, cin * 256:(cin + 1) * 256],
                                          start=False, stop=(ci == order[-1]))
                        ft = inc("TE", mm)
                        wait("SE", ft)
                        cchunk = chi * 4 + co
                        f1_free[pidx] = inc("SE", SC.copy(
                            xfA[:, cchunk * 256:(cchunk + 1) * 256], ps[:, :]))
                wait("VE", tok("SE"))
                a4 = xfA[:, :].rearrange("p (g c n) -> p g c n", g=2, c=8)
                b4 = xfB[:, :].rearrange("p (g c n) -> p g c n", g=2, c=8)
                a_lo = xfA[:, :].rearrange("p (g c n) -> p g c n", g=4, c=4)
                b_lo = xfB[:, :].rearrange("p (g c n) -> p g c n", g=4, c=4)
                ev_in0 = a_lo[:, 0::2, :, :]
                ev_in1 = a_lo[:, 1::2, :, :]
                V.tensor_tensor(b_lo[:, 0::2, :, :], ev_in0, ev_in1, ALU.add)
                V.tensor_tensor(b_lo[:, 1::2, :, :], ev_in0, ev_in1, ALU.subtract)
                V.tensor_tensor(a4[:, 0, :, :], b4[:, 0, :, :], b4[:, 1, :, :],
                                ALU.add)
                V.tensor_tensor(a4[:, 1, :, :], b4[:, 0, :, :], b4[:, 1, :, :],
                                ALU.subtract)
                V.memset(xfA[0:1, 0:256], 0.0)
                vt = inc("VE", V.reduce_max(
                    mm1[:, :],
                    xfA[:, :].rearrange("p (c n) -> p n c", c=NCH),
                    axis=AX.X, apply_absolute_value=True))
                V.drain()
                V.tensor_copy(mmh[:, :], mm1[:, :])
                vt = inc("VE", V.tensor_tensor(mml[:, :], mm1[:, :], mmh[:, :],
                                               ALU.subtract))
                wait("TE", vt)
                T.transpose(ps_txb[:, 0:128], mmh[:, 0:128], identb[:, :])
                T.transpose(ps_txb[:, 128:256], mmh[:, 128:256], identb[:, :])
                T.transpose(ps_txb[:, 256:384], mml[:, 0:128], identb[:, :])
                tt = inc("TE", T.transpose(ps_txb[:, 384:512], mml[:, 128:256],
                                           identb[:, :]))
                wait("VE", tt)
                V.tensor_copy(scf[:, 0:256], ps_txb[:, 0:256])
                V.tensor_tensor(scf[:, 0:256], scf[:, 0:256], ps_txb[:, 256:512],
                                ALU.add)
                cols = slice(s5 * 2, s5 * 2 + 2)
                V.reduce_max(r1_grid[:, cols],
                             scf[:, 0:256].rearrange("p (s n) -> p s n", s=2),
                             axis=AX.X)
                V.drain()
                V.tensor_scalar(tinyf[:, 0:2], r1_grid[:, cols], 1.0, 1e-30,
                                ALU.mult, ALU.max)
                V.reciprocal(tinyf[:, 2:4], tinyf[:, 0:2])
                V.drain()
                vs = inc("VE", V.tensor_scalar(tinyf[:, 4:6], tinyf[:, 2:4], 127.0,
                                               0.0, ALU.mult, ALU.add))
                V.tensor_copy(tinyb[:, 0:2], tinyf[:, 4:6])
                V.tensor_copy(tinyf[:, 6:8], tinyb[:, 0:2])
                vt2 = inc("VE", V.tensor_tensor(tinyf[:, 8:10], tinyf[:, 4:6],
                                                tinyf[:, 6:8], ALU.subtract))
                sc_tr = None
                for seg in range(2):
                    wait("VE", sc_tr, vt2)
                    V.tensor_scalar_mul(bth[:, seg * 128:(seg + 1) * 128],
                                        onesb[:, :], tinyf[:, 6 + seg:7 + seg])
                    bt = inc("VE", V.tensor_scalar_mul(
                        btl[:, seg * 128:(seg + 1) * 128], onesb[:, :],
                        tinyf[:, 8 + seg:9 + seg]))
                    wait("TE", bt)
                    T.transpose(ps_txb[:, seg * 128:(seg + 1) * 128],
                                bth[:, seg * 128:(seg + 1) * 128], identb[:, :])
                    sc_tr = inc("TE", T.transpose(
                        ps_txb[:, 256 + seg * 128:256 + (seg + 1) * 128],
                        btl[:, seg * 128:(seg + 1) * 128], identb[:, :]))
                wait("VE", sc_tr)
                V.tensor_copy(scf[:, 0:256], ps_txb[:, 0:256])
                V.tensor_tensor(scf[:, 0:256], scf[:, 0:256], ps_txb[:, 256:512],
                                ALU.add)
                V.tensor_tensor(
                    xfB[:, :].rearrange("p (c n) -> p c n", c=NCH),
                    xfA[:, :].rearrange("p (c n) -> p c n", c=NCH),
                    scf[:, 0:256].rearrange("p (o n) -> p o n", o=1).broadcast_to([128, NCH, 256]),
                    ALU.mult)
                V.tensor_scalar(xfB[:, :], xfB[:, :], MAGIC, MAGIC,
                                ALU.add, ALU.subtract)
                V.tensor_copy(i32a[:, :], xfB[:, :])
                q1t = inc("VE", V.tensor_copy(
                    xq1[:, :].rearrange("p (c n) -> p c n", c=NCH)[
                        :, :, sl2 * 256:(sl2 + 1) * 256],
                    i32a[:, :].rearrange("p (c n) -> p c n", c=NCH)))
            for fc in range(NCH2):
                wait("GP", wt_tok)
                wl = inc("GP", GP.dma_start(
                    wup_b[fc % 2][:, :].rearrange("p (d c) -> p d c", d=NCH),
                    wupt_hbm[:, :, fc * 128:(fc + 1) * 128].rearrange(
                        "d p c -> p d c")))
                pu = ps_up[fc % 2]
                wait("TE", wl, q1t, tok("SE"))
                for dc in range(NCH):
                    mm = T.matmul(pu[:, :],
                                  wup_b[fc % 2][:, dc * 128:(dc + 1) * 128],
                                  xq1[:, dc * 512:(dc + 1) * 512],
                                  start=(dc == 0), stop=(dc == NCH - 1))
                um = inc("TE", mm)
                wt_tok = um
                wait("SE", um, tok("VE"))
                SC.activation(grel[:, :], pu[:, :], AF.Relu)
                sg = inc("SE", SC.activation(g2[:, :], grel[:, :], AF.Square))
                wait("VE", sg, g_store)
                V.tensor_copy(ghi_sb[:, fc * 512:(fc + 1) * 512], g2[:, :])
                inc("VE", V.tensor_tensor(glo_sb[:, fc * 512:(fc + 1) * 512],
                                          g2[:, :],
                                          ghi_sb[:, fc * 512:(fc + 1) * 512],
                                          ALU.subtract))
            wait("LD", tok("VE"))
            inc("LD", SY.dma_start(ghi_hbm[grp, :, :], ghi_sb[:, :]))
            g_store = inc("LD", SY.dma_start(glo_hbm[grp, :, :], glo_sb[:, :]))
            if env["dbg_ghi_ap"] is not None:
                g_store = inc("LD", SY.dma_start(env["dbg_ghi_ap"][grp, :, :],
                                                 ghi_sb[:, :]))
    barrier()

    sb5 = ExitStack()
    gh_sl = sb5.enter_context(nc.sbuf_tensor(_nm("gh_sl"), [128, NCH2 * 256], BF16))
    gl_sl = sb5.enter_context(nc.sbuf_tensor(_nm("gl_sl"), [128, NCH2 * 256], BF16))
    yfA = sb5.enter_context(nc.sbuf_tensor(_nm("yfA"), [128, NCH2 * 256], F32))
    yfB = sb5.enter_context(nc.sbuf_tensor(_nm("yfB"), [128, NCH2 * 256], F32))
    i32b = sb5.enter_context(nc.sbuf_tensor(_nm("i32b"), [128, NCH2 * 256], I32))
    xq2 = sb5.enter_context(nc.sbuf_tensor(_nm("xq2"), [128, NCH2 * 512], BF16))
    mm2 = sb5.enter_context(nc.sbuf_tensor(_nm("mm2"), [128, 256], F32))
    mmh2 = sb5.enter_context(nc.sbuf_tensor(_nm("mmh2"), [128, 256], BF16))
    mml2 = sb5.enter_context(nc.sbuf_tensor(_nm("mml2"), [128, 256], BF16))
    scf2 = sb5.enter_context(nc.sbuf_tensor(_nm("scf2"), [128, 256], F32))
    tinyb2 = sb5.enter_context(nc.sbuf_tensor(_nm("tinyb2"), [128, 2], BF16))
    bth2 = sb5.enter_context(nc.sbuf_tensor(_nm("bth2"), [128, 256], BF16))
    btl2 = sb5.enter_context(nc.sbuf_tensor(_nm("btl2"), [128, 256], BF16))
    wdn_b = [sb5.enter_context(nc.sbuf_tensor(_nm(f"wdn_{i}"), [128, 1024], BF16))
             for i in range(2)]
    par_sb = sb5.enter_context(nc.sbuf_tensor(_nm("par_sb"), [128, 1024], F32))
    out_sb = sb5.enter_context(nc.sbuf_tensor(_nm("out_sb"), [128, 1024], F32))
    with sb5:
        for grp in range(4):
            f2 = ExitStack()
            ps_f2 = [f2.enter_context(nc.psum_tensor(_nm(f"ps_f2{i}"), [128, 256], F32))
                     for i in range(3)]
            ps_txb2 = f2.enter_context(nc.psum_tensor(_nm("ps_tx2"), [128, 512], BF16))
            with f2:
                for sl2 in range(2):
                    s5 = grp * 2 + sl2
                    wait("LD", tok("VE"), tok("TE"))
                    inc("LD", SY.dma_start(
                        gh_sl[:, :].rearrange("p (c n) -> p c n", c=NCH2),
                        ghi_hbm[grp].rearrange("p (c m) -> p c m", c=NCH2)[
                            :, :, sl2 * 256:(sl2 + 1) * 256]))
                    lg = inc("LD", SY.dma_start(
                        gl_sl[:, :].rearrange("p (c n) -> p c n", c=NCH2),
                        glo_hbm[grp].rearrange("p (c m) -> p c m", c=NCH2)[
                            :, :, sl2 * 256:(sl2 + 1) * 256]))
                    wait("TE", lg)
                    f2_free = [None, None, None]
                    for chi in range(8):
                        for co in range(4):
                            pidx = (chi * 4 + co) % 3
                            ps = ps_f2[pidx]
                            wait("TE", f2_free[pidx])
                            order = sorted(range(4), key=lambda ci: -H4[co][ci])
                            first = True
                            for ci in order:
                                lhs = h128 if H4[co][ci] > 0 else h128n
                                cin = chi * 4 + ci
                                T.matmul(ps[:, :], lhs[:, :],
                                         gh_sl[:, cin * 256:(cin + 1) * 256],
                                         start=first, stop=False)
                                first = False
                                mm = T.matmul(ps[:, :], lhs[:, :],
                                              gl_sl[:, cin * 256:(cin + 1) * 256],
                                              start=False, stop=(ci == order[-1]))
                            ft = inc("TE", mm)
                            wait("SE", ft)
                            cchunk = chi * 4 + co
                            f2_free[pidx] = inc("SE", SC.copy(
                                yfA[:, cchunk * 256:(cchunk + 1) * 256], ps[:, :]))
                    wait("VE", tok("SE"))
                    a_s4 = yfA[:, :].rearrange("p (g c n) -> p g c n", g=4, c=8)
                    b_s4 = yfB[:, :].rearrange("p (g c n) -> p g c n", g=4, c=8)
                    a4i0 = yfA[:, :].rearrange("p (g c n) -> p g c n", g=8, c=4)
                    b4i0 = yfB[:, :].rearrange("p (g c n) -> p g c n", g=8, c=4)
                    V.tensor_tensor(b4i0[:, 0::2, :, :], a4i0[:, 0::2, :, :],
                                    a4i0[:, 1::2, :, :], ALU.add)
                    V.tensor_tensor(b4i0[:, 1::2, :, :], a4i0[:, 0::2, :, :],
                                    a4i0[:, 1::2, :, :], ALU.subtract)
                    V.tensor_tensor(a_s4[:, 0::2, :, :], b_s4[:, 0::2, :, :],
                                    b_s4[:, 1::2, :, :], ALU.add)
                    V.tensor_tensor(a_s4[:, 1::2, :, :], b_s4[:, 0::2, :, :],
                                    b_s4[:, 1::2, :, :], ALU.subtract)
                    a16 = yfA[:, :].rearrange("p (g c n) -> p g c n", g=2, c=16)
                    b16 = yfB[:, :].rearrange("p (g c n) -> p g c n", g=2, c=16)
                    V.tensor_tensor(b16[:, 0, :, :], a16[:, 0, :, :],
                                    a16[:, 1, :, :], ALU.add)
                    V.tensor_tensor(b16[:, 1, :, :], a16[:, 0, :, :],
                                    a16[:, 1, :, :], ALU.subtract)
                    vt = inc("VE", V.reduce_max(
                        mm2[:, :],
                        yfB[:, :].rearrange("p (c n) -> p n c", c=NCH2),
                        axis=AX.X, apply_absolute_value=True))
                    V.drain()
                    V.tensor_copy(mmh2[:, :], mm2[:, :])
                    vt = inc("VE", V.tensor_tensor(mml2[:, :], mm2[:, :], mmh2[:, :],
                                                   ALU.subtract))
                    wait("TE", vt)
                    T.transpose(ps_txb2[:, 0:128], mmh2[:, 0:128], identb[:, :])
                    T.transpose(ps_txb2[:, 128:256], mmh2[:, 128:256], identb[:, :])
                    T.transpose(ps_txb2[:, 256:384], mml2[:, 0:128], identb[:, :])
                    tt = inc("TE", T.transpose(ps_txb2[:, 384:512], mml2[:, 128:256],
                                               identb[:, :]))
                    wait("VE", tt)
                    V.tensor_copy(scf2[:, 0:256], ps_txb2[:, 0:256])
                    V.tensor_tensor(scf2[:, 0:256], scf2[:, 0:256],
                                    ps_txb2[:, 256:512], ALU.add)
                    cols = slice(s5 * 2, s5 * 2 + 2)
                    V.reduce_max(tinyf[:, 6:8],
                                 scf2[:, 0:256].rearrange("p (s n) -> p s n", s=2),
                                 axis=AX.X)
                    V.drain()
                    V.tensor_scalar(tinyf[:, 0:2], tinyf[:, 6:8], 1.0, 1e-30,
                                    ALU.mult, ALU.max)
                    V.reciprocal(tinyf[:, 2:4], tinyf[:, 0:2])
                    V.drain()
                    vs = inc("VE", V.tensor_scalar(tinyf[:, 4:6], tinyf[:, 2:4],
                                                   127.0, 0.0, ALU.mult, ALU.add))
                    V.tensor_tensor(tinyf[:, 8:10], s_own[:, cols],
                                    r1_grid[:, cols], ALU.mult)
                    V.tensor_scalar(tinyf[:, 8:10], tinyf[:, 8:10],
                                    1.0 / math.sqrt(D), EPS_FFN, ALU.mult, ALU.max)
                    V.tensor_tensor(tinyf[:, 8:10], tinyf[:, 8:10],
                                    wsu_col[:, :].broadcast_to([128, 2]), ALU.mult)
                    V.tensor_scalar(tinyf[:, 8:10], tinyf[:, 8:10], 1.0 / 127.0,
                                    0.0, ALU.mult, ALU.add)
                    V.tensor_tensor(tinyf[:, 10:12], tinyf[:, 8:10],
                                    tinyf[:, 8:10], ALU.mult)
                    V.tensor_tensor(tinyf[:, 10:12], tinyf[:, 10:12],
                                    tinyf[:, 6:8], ALU.mult)
                    V.tensor_scalar(tinyf[:, 10:12], tinyf[:, 10:12],
                                    1.0 / (4.0 * math.sqrt(FI)), EPS_FFN,
                                    ALU.mult, ALU.max)
                    V.tensor_tensor(tinyf[:, 10:12], tinyf[:, 10:12],
                                    wsd_col[:, :].broadcast_to([128, 2]), ALU.mult)
                    inc("VE", V.tensor_scalar(fs_grid[:, cols], tinyf[:, 10:12],
                                              1.0 / 254.0, 0.0, ALU.mult, ALU.add))
                    V.tensor_copy(tinyb2[:, 0:2], tinyf[:, 4:6])
                    V.tensor_copy(tinyf[:, 12:14], tinyb2[:, 0:2])
                    vt2 = inc("VE", V.tensor_tensor(tinyf[:, 14:16], tinyf[:, 4:6],
                                                    tinyf[:, 12:14], ALU.subtract))
                    sc_tr = None
                    for seg in range(2):
                        wait("VE", sc_tr, vt2)
                        V.tensor_scalar_mul(bth2[:, seg * 128:(seg + 1) * 128],
                                            onesb[:, :], tinyf[:, 12 + seg:13 + seg])
                        bt = inc("VE", V.tensor_scalar_mul(
                            btl2[:, seg * 128:(seg + 1) * 128], onesb[:, :],
                            tinyf[:, 14 + seg:15 + seg]))
                        wait("TE", bt)
                        T.transpose(ps_txb2[:, seg * 128:(seg + 1) * 128],
                                    bth2[:, seg * 128:(seg + 1) * 128], identb[:, :])
                        sc_tr = inc("TE", T.transpose(
                            ps_txb2[:, 256 + seg * 128:256 + (seg + 1) * 128],
                            btl2[:, seg * 128:(seg + 1) * 128], identb[:, :]))
                    wait("VE", sc_tr)
                    V.tensor_copy(scf2[:, 0:256], ps_txb2[:, 0:256])
                    V.tensor_tensor(scf2[:, 0:256], scf2[:, 0:256],
                                    ps_txb2[:, 256:512], ALU.add)
                    V.tensor_tensor(
                        yfA[:, :].rearrange("p (c n) -> p c n", c=NCH2),
                        yfB[:, :].rearrange("p (c n) -> p c n", c=NCH2),
                        scf2[:, 0:256].rearrange("p (o n) -> p o n", o=1).broadcast_to(
                            [128, NCH2, 256]),
                        ALU.mult)
                    V.tensor_scalar(yfA[:, :], yfA[:, :], MAGIC, MAGIC,
                                    ALU.add, ALU.subtract)
                    V.tensor_copy(i32b[:, :], yfA[:, :])
                    q2t = inc("VE", V.tensor_copy(
                        xq2[:, :].rearrange("p (c n) -> p c n", c=NCH2)[
                            :, :, sl2 * 256:(sl2 + 1) * 256],
                        i32b[:, :].rearrange("p (c n) -> p c n", c=NCH2)))
            dn = ExitStack()
            ps_dn = [dn.enter_context(nc.psum_tensor(_nm(f"ps_dn{i}"), [128, 1024], F32))
                     for i in range(4)]
            with dn:
                wd_tok = None
                par_tok = None
                for half in range(2):
                    for fc in range(NCH2):
                        wait("GP", wd_tok)
                        wl = inc("GP", GP.dma_start(
                            wdn_b[fc % 2][:, :],
                            wdnt_hbm[fc, :, half * 1024:(half + 1) * 1024]))
                        wait("TE", wl, q2t, tok("VE"))
                        for lt2 in range(4):
                            for sl in range(2):
                                mm = T.matmul(
                                    ps_dn[lt2][:, sl * 512:(sl + 1) * 512],
                                    xq2[:, fc * 512 + lt2 * 128:fc * 512 + (lt2 + 1) * 128],
                                    wdn_b[fc % 2][:, sl * 512:(sl + 1) * 512],
                                    start=(fc == 0), stop=(fc == NCH2 - 1))
                        wd_tok = inc("TE", mm)
                    for lt2 in range(4):
                        i = grp * 4 + lt2
                        wait("LD", par_tok, tok("VE"))
                        pl = inc("LD", SY.dma_start(
                            par_sb[:, :],
                            partial_hbm[i, :, half * 1024:(half + 1) * 1024]))
                        wait("VE", wd_tok, pl, par_tok)
                        vt = inc("VE", V.scalar_tensor_tensor(
                            out_sb[:, :], ps_dn[lt2][:, :], fs_grid[:, i:i + 1],
                            par_sb[:, :], ALU.mult, ALU.add))
                        wait("LD", vt)
                        par_tok = inc("LD", SY.dma_start(
                            out_hbm[i, :, half * 1024:(half + 1) * 1024],
                            out_sb[:, :]))
                    wait("TE", tok("VE"))


_PROGRAM_CACHE = {}


def _get_program():
    key = STAGES
    if key not in _PROGRAM_CACHE:
        _PROGRAM_CACHE[key] = build_program()
    return _PROGRAM_CACHE[key]


def _hadamard(n):
    h = np.array([[1.0]], dtype=np.float64)
    while h.shape[0] < n:
        h = np.block([[h, h], [h, -h]])
    return h


def _bf(x):
    return np.ascontiguousarray(np.asarray(x).astype(ml_dtypes.bfloat16))


def _f32c(x):
    return np.ascontiguousarray(np.asarray(x, dtype=np.float32))


def make_in_maps(inputs):
    x = np.asarray(inputs["L0_attn_x"], dtype=np.float32)
    Wq = np.asarray(inputs["L0_attn_Wq"], dtype=np.float32)
    bq = np.asarray(inputs["L0_attn_bq"], dtype=np.float32)
    Wk = np.asarray(inputs["L0_attn_Wk"], dtype=np.float32)
    bk = np.asarray(inputs["L0_attn_bk"], dtype=np.float32)
    Wv = np.asarray(inputs["L0_attn_Wv"], dtype=np.float32)
    bv = np.asarray(inputs["L0_attn_bv"], dtype=np.float32)
    Wo = np.asarray(inputs["L0_attn_Wo"], dtype=np.float32)
    bo = np.asarray(inputs["L0_attn_bo"], dtype=np.float32)
    qg = np.asarray(inputs["L0_attn_q_gain"], dtype=np.float32)
    w_up = np.asarray(inputs["L0_ffn_w_up"], dtype=np.float32)
    w_dn = np.asarray(inputs["L0_ffn_w_down"], dtype=np.float32)

    wqT = _f32c(Wq.T)
    wkT = _f32c(Wk.T)
    wvT = _f32c(Wv.T)
    woT = _f32c(Wo.T)
    wupT = _f32c(w_up.T)
    wdnT = _f32c(w_dn.T)
    bkv = _f32c(np.concatenate([bk, bv])[None, :])

    inv_freq = 1.0 / (10000.0 ** (np.arange(0, HD, 2, dtype=np.float64) / HD))
    tpos = np.arange(S, dtype=np.float64)
    fr = np.outer(tpos, inv_freq)
    cosT = np.cos(fr).reshape(NT, 128, 64)
    sinT = np.sin(fr).reshape(NT, 128, 64)

    Hm = _hadamard(128)
    tri = np.zeros((128, 128), dtype=np.float64)
    kk, qq = np.meshgrid(np.arange(128), np.arange(128), indexing="ij")
    tri[kk > qq] = -30000.0

    in_maps = []
    for core in range(8):
        b, p = core // 2, core % 2
        xb = x[b]
        own = np.arange(p, NT, 2)
        x_ownv = np.ascontiguousarray(xb.reshape(NT, 128, D)[own])
        mask_e = tri if p == 0 else np.zeros((128, 128))
        mask_o = np.full((128, 128), -30000.0) if p == 0 else tri
        m = dict(
            x_tok=_f32c(xb),
            xT_own=_f32c(x_ownv.transpose(2, 0, 1).reshape(D, OT * 128)),
            x_own=_f32c(x_ownv),
            wqT=wqT, wkT=wkT, wvT=wvT, woT=woT, wupT=wupT, wdnT=wdnT,
            bq=_f32c(bq[None, :]), bkv=bkv, bo=_f32c(bo[None, :]),
            qgain=_f32c(qg[None, :]),
            cos_all=_f32c(cosT), sin_all=_f32c(sinT),
            cos_own=_f32c(cosT[own]), sin_own=_f32c(sinT[own]),
            h128_in=_bf(Hm), h128n_in=_bf(-Hm),
            identb_in=_bf(np.eye(128)), identf_in=_f32c(np.eye(128)),
            mask_e=_bf(mask_e), mask_o=_bf(mask_o),
        )
        in_maps.append(m)
    return in_maps


def kernel(**inputs):
    nc = _get_program()
    in_maps = make_in_maps(inputs)
    res = run_bass_kernel_spmd(nc, in_maps, core_ids=list(range(8)))
    out = np.zeros((B, S, D), dtype=np.float32)
    for core in range(8):
        b, p = core // 2, core % 2
        o = res.results[core]["out"]
        out.reshape(B, NT, 128, D)[b, np.arange(p, NT, 2)] = o
    return out


# version 03 (final)
# speedup vs baseline: 1.0119x; 1.0119x over previous
import math
import numpy as np
import ml_dtypes
from contextlib import ExitStack

import concourse.bass as bass
import concourse.mybir as mybir
from concourse.bass_utils import run_bass_kernel_spmd

F32 = mybir.dt.float32
BF16 = mybir.dt.bfloat16
I32 = mybir.dt.int32
AF = mybir.ActivationFunctionType
ALU = mybir.AluOpType
AX = mybir.AxisListType

B, S, D = 4, 4096, 2048
H, KVH, HD = 16, 4, 128
KVD = KVH * HD
FI = 4096
NT = S // 128
OT = NT // 2
NCH = D // 128
NCH2 = FI // 128
EPS_LN = 1e-5
EPS_RMS = 1.1920929e-07
EPS_FFN = 1e-5
SQHD = math.sqrt(HD)
MAGIC = 1.5 * 2.0 ** 23

STAGES = 5
DO_S2 = True
TERN_PART = 3
DO_TERN = True
DO_S3 = True
DO_S5 = True
DBG = False

_UID = [0]
def _nm(base):
    _UID[0] += 1
    return f"{base}_{_UID[0]}"


class _VGuard:

    _PASS = ("wait_ge", "wait_op", "sem_inc", "br", "nop", "halt",
             "dbg_callback", "breakpoint")

    def __init__(self, eng):
        self._e = eng
        self._hist = []
        self.n_drains = 0

    def __getattr__(self, name):
        e = self.__dict__["_e"]
        fn = getattr(e, name)
        if not callable(fn) or name.startswith("_"):
            return fn
        hist = self.__dict__["_hist"]
        if name == "drain":
            def g(*a, **k):
                hist.clear()
                return fn(*a, **k)
            return g
        if name in _VGuard._PASS:
            return fn

        def g(*a, **k):
            aps = [x for x in a if hasattr(x, "tensor")]
            outs, ins = set(), set()
            if aps:
                outs.add(aps[0].tensor.name)
                for x in aps[1:]:
                    ins.add(x.tensor.name)
            for kw, val in k.items():
                if hasattr(val, "tensor"):
                    if kw == "accum_out":
                        outs.add(val.tensor.name)
                    else:
                        ins.add(val.tensor.name)
            if any(ins & h for h in hist):
                e.drain()
                hist.clear()
                self.n_drains += 1
            r = fn(*a, **k)
            hist.append(outs)
            if len(hist) > 2:
                hist.pop(0)
            return r
        return g


def build_program():
    nc = bass.Bass("TRN2", target_bir_lowering=False, detect_race_conditions=False)

    def DI(name, shape, dt=F32):
        return nc.dram_tensor(name, shape, dt, kind="ExternalInput").ap()

    x_tok = DI("x_tok", [S, D])
    xT_own = DI("xT_own", [D, OT * 128])
    x_own = DI("x_own", [OT, 128, D])
    wqT = DI("wqT", [D, D])
    wkT = DI("wkT", [D, KVD])
    wvT = DI("wvT", [D, KVD])
    woT = DI("woT", [D, D])
    wupT = DI("wupT", [D, FI])
    wdnT = DI("wdnT", [FI, D])
    bq_i = DI("bq", [1, D])
    bkv_i = DI("bkv", [1, 2 * KVD])
    bo_i = DI("bo", [1, D])
    qg_i = DI("qgain", [1, H])
    cos_all = DI("cos_all", [NT, 128, 64], F32)
    sin_all = DI("sin_all", [NT, 128, 64], F32)
    cos_own = DI("cos_own", [OT, 128, 64], F32)
    sin_own = DI("sin_own", [OT, 128, 64], F32)
    h128_i = DI("h128_in", [128, 128], BF16)
    h128n_i = DI("h128n_in", [128, 128], BF16)
    identb_i = DI("identb_in", [128, 128], BF16)
    identf_i = DI("identf_in", [128, 128], F32)
    maske_i = DI("mask_e", [128, 128], BF16)
    masko_i = DI("mask_o", [128, 128], BF16)

    partial_hbm = nc.dram_tensor("partial_hbm", [OT, 128, D], F32, kind="Internal").ap()
    ghi_hbm = nc.dram_tensor("ghi_hbm", [4, 128, NCH2 * 512], BF16, kind="Internal").ap()
    glo_hbm = nc.dram_tensor("glo_hbm", [4, 128, NCH2 * 512], BF16, kind="Internal").ap()
    wupt_hbm = nc.dram_tensor("wupt_hbm", [NCH, 128, FI], BF16, kind="Internal").ap()
    wdnt_hbm = nc.dram_tensor("wdnt_hbm", [NCH2, 128, D], BF16, kind="Internal").ap()
    out_hbm = nc.dram_tensor("out", [OT, 128, D], F32, kind="ExternalOutput").ap()
    if DBG:
        dbg_partial_ap = nc.dram_tensor("dbg_partial", [OT, 128, D], F32,
                                        kind="ExternalOutput").ap()
        dbg_ghi_ap = nc.dram_tensor("dbg_ghi", [4, 128, NCH2 * 512], BF16,
                                    kind="ExternalOutput").ap()
        dbg_grids_ap = nc.dram_tensor("dbg_grids", [128, 80], F32,
                                      kind="ExternalOutput").ap()
        dbg_tern_ap = nc.dram_tensor("dbg_tern", [128, 36], F32,
                                     kind="ExternalOutput").ap()
    else:
        dbg_partial_ap = dbg_ghi_ap = dbg_grids_ap = dbg_tern_ap = None

    es = ExitStack()
    names = ["LD", "GP", "TE", "VE", "SE"]
    sem = {n: es.enter_context(nc.semaphore(n)) for n in names}
    cnt = {n: 0 for n in names}
    eng = {"LD": nc.sync, "GP": nc.gpsimd, "TE": nc.tensor, "VE": nc.vector,
           "SE": nc.scalar}
    waited = {n: {m: 0 for m in names} for n in names}

    def inc(name, instr, n=None):
        if n is None:
            n = 16 if name in ("LD", "GP") else 1
        instr.then_inc(sem[name], n)
        cnt[name] += n
        return (name, cnt[name])

    def tok(name):
        return (name, cnt[name])

    def wait(who, *tokens):
        e = eng[who]
        for t in tokens:
            if t is None:
                continue
            name, val = t
            if name == who:
                continue
            if val > waited[who][name]:
                e.wait_ge(sem[name], val)
                waited[who][name] = val

    def barrier(whos=None):
        toks = [(n, cnt[n]) for n in names]
        for w in (whos or names):
            wait(w, *toks)

    V = _VGuard(nc.vector)
    SC, T, SY, GP = nc.scalar, nc.tensor, nc.sync, nc.gpsimd

    identb = es.enter_context(nc.sbuf_tensor(_nm("identb"), [128, 128], BF16))
    identf = es.enter_context(nc.sbuf_tensor(_nm("identf"), [128, 128], F32))
    h128 = es.enter_context(nc.sbuf_tensor(_nm("h128"), [128, 128], BF16))
    h128n = es.enter_context(nc.sbuf_tensor(_nm("h128n"), [128, 128], BF16))
    onesb = es.enter_context(nc.sbuf_tensor(_nm("onesb"), [128, 128], BF16))
    onestok = es.enter_context(nc.sbuf_tensor(_nm("onestok"), [128, 128], F32))
    maskes = es.enter_context(nc.sbuf_tensor(_nm("maskes"), [128, 128], BF16))
    maskos = es.enter_context(nc.sbuf_tensor(_nm("maskos"), [128, 128], BF16))
    bgrid = es.enter_context(nc.sbuf_tensor(_nm("bgrid"), [128, H], F32))
    qgg = es.enter_context(nc.sbuf_tensor(_nm("qgg"), [128, H], F32))
    s_own = es.enter_context(nc.sbuf_tensor(_nm("s_own"), [128, OT], F32))
    r1_grid = es.enter_context(nc.sbuf_tensor(_nm("r1_grid"), [128, OT], F32))
    fs_grid = es.enter_context(nc.sbuf_tensor(_nm("fs_grid"), [128, OT], F32))
    wsu_col = es.enter_context(nc.sbuf_tensor(_nm("wsu_col"), [128, 1], F32))
    wsd_col = es.enter_context(nc.sbuf_tensor(_nm("wsd_col"), [128, 1], F32))
    bq_pad = es.enter_context(nc.sbuf_tensor(_nm("bq_pad"), [128, D], BF16))
    bkv_pad = es.enter_context(nc.sbuf_tensor(_nm("bkv_pad"), [128, 2 * KVD], BF16))
    bo_pad = es.enter_context(nc.sbuf_tensor(_nm("bo_pad"), [128, D], BF16))
    padf = es.enter_context(nc.sbuf_tensor(_nm("padf"), [128, 2], F32))
    bshadow = es.enter_context(nc.sbuf_tensor(_nm("bshadow"), [128, H], F32))
    dbgT = es.enter_context(nc.sbuf_tensor(_nm("dbgT"), [128, 36], F32))
    tinyf = es.enter_context(nc.sbuf_tensor(_nm("tinyf"), [128, 16], F32))

    with es:
        inc("LD", SY.dma_start(identb[:, :], identb_i))
        inc("LD", SY.dma_start(identf[:, :], identf_i))
        inc("LD", SY.dma_start(h128[:, :], h128_i))
        inc("LD", SY.dma_start(h128n[:, :], h128n_i))
        inc("LD", SY.dma_start(maskes[:, :], maske_i))
        inc("LD", SY.dma_start(maskos[:, :], masko_i))
        V.memset(onesb[:, :], 1.0)
        V.memset(onestok[:, :], 1.0)
        V.memset(bq_pad[:, :], 0.0)
        V.memset(bkv_pad[:, :], 0.0)
        V.memset(bo_pad[:, :], 0.0)
        V.memset(dbgT[:, :], 0.0)
        inc("VE", V.memset(padf[:, :], 0.0))

        with nc.sbuf_tensor(_nm("qg_row"), [1, H], F32) as qg_row, \
             nc.sbuf_tensor(_nm("qg_pad"), [128, 2 * H], BF16) as qg_pad, \
             nc.psum_tensor(_nm("ps_qg"), [128, 2 * H], F32) as ps_qg:
            inc("LD", SY.dma_start(qg_row[:, :], qg_i))
            wait("VE", tok("LD"))
            V.memset(qg_pad[:, :], 0.0)
            inc("VE", V.tensor_copy(qg_pad[0:1, 0:H], qg_row[:, :]))
            wait("SE", tok("VE"), tok("LD"))
            inc("SE", SC.activation(qg_pad[0:1, H:2 * H], qg_row[:, :], AF.Abs))
            wait("TE", tok("SE"), tok("VE"))
            inc("TE", T.matmul(ps_qg[:, :], onesb[:, :], qg_pad[:, :],
                               start=True, stop=True))
            wait("VE", tok("TE"))
            V.tensor_scalar(qgg[:, :], ps_qg[:, 0:H], 1.0 / SQHD, 0.0,
                            ALU.mult, ALU.add)
            V.tensor_scalar(bgrid[:, :], ps_qg[:, H:2 * H], -SQHD, 0.0,
                            ALU.mult, ALU.add)
            inc("VE", V.tensor_copy(bshadow[:, :], bgrid[:, :]))
        if DBG:
            dbg_early_ap = nc.dram_tensor("dbg_early", [128, H], F32,
                                          kind="ExternalOutput").ap()
            wait("LD", tok("VE"))
            inc("LD", SY.dma_start(dbg_early_ap, bgrid[:, :]))
        barrier()
        with nc.sbuf_tensor(_nm("brow"), [1, 2 * D + 2 * KVD], F32) as brow:
            inc("LD", SY.dma_start(brow[:, 0:D], bq_i))
            inc("LD", SY.dma_start(brow[:, D:D + 2 * KVD], bkv_i))
            inc("LD", SY.dma_start(brow[:, D + 2 * KVD:2 * D + 2 * KVD], bo_i))
            wait("VE", tok("LD"))
            V.tensor_copy(bq_pad[0:1, :], brow[:, 0:D])
            V.tensor_copy(bkv_pad[0:1, :], brow[:, D:D + 2 * KVD])
            inc("VE", V.tensor_copy(bo_pad[0:1, :],
                                    brow[:, D + 2 * KVD:2 * D + 2 * KVD]))
        barrier(["TE", "VE", "SE", "LD", "GP"])

        def ternarize(wT_ap, dst_hbm, ntiles, ncols, ws_col, use_sign):
            tes = ExitStack()
            wa = tes.enter_context(nc.sbuf_tensor(_nm("tn_wa"), [128, ncols], F32))
            wb = tes.enter_context(nc.sbuf_tensor(_nm("tn_wb"), [128, ncols], F32))
            tmpf = tes.enter_context(nc.sbuf_tensor(_nm("tn_tmp"), [128, ncols], F32))
            scrap = tes.enter_context(nc.sbuf_tensor(_nm("tn_scrap"), [128, ncols], BF16))
            tern = tes.enter_context(nc.sbuf_tensor(_nm("tn_t"), [128, ncols], BF16))
            terns = tes.enter_context(nc.sbuf_tensor(_nm("tn_t2"), [128, ncols], BF16))
            i32s = tes.enter_context(nc.sbuf_tensor(_nm("tn_i32"), [128, ncols], I32))
            absg = tes.enter_context(nc.sbuf_tensor(_nm("tn_absg"), [128, NCH2], F32))
            hcol = tes.enter_context(nc.sbuf_tensor(_nm("tn_h"), [128, 1], F32))
            nhcol = tes.enter_context(nc.sbuf_tensor(_nm("tn_nh"), [128, 1], F32))
            ps_w = tes.enter_context(nc.psum_tensor(_nm("tn_ps"), [128, 256], F32))
            ps_wb = tes.enter_context(nc.psum_tensor(_nm("tn_psb"), [128, 256], BF16))
            hilo = tes.enter_context(nc.sbuf_tensor(_nm("tn_hilo"), [128, 8], BF16))
            absgb = tes.enter_context(nc.sbuf_tensor(_nm("tn_absgb"), [128, 256], BF16))
            hcl = tes.enter_context(nc.sbuf_tensor(_nm("tn_hcl"), [128, 32], F32))
            with tes:
                bufs, freed = [wa, wb], [None, None]
                for r in range(ntiles):
                    bi = r % 2
                    wait("LD", freed[bi])
                    lt = inc("LD", SY.dma_start(bufs[bi][:, :],
                                                wT_ap[r * 128:(r + 1) * 128, :]))
                    wait("SE", lt)
                    if TERN_PART >= 1:
                        freed[bi] = inc("SE", SC.activation(
                            scrap[:, :], bufs[bi][:, :], AF.Abs,
                            accum_out=absg[:, r:r + 1]))
                    else:
                        freed[bi] = inc("SE", SC.copy(scrap[:, 0:128],
                                                      bufs[bi][:, 0:128]))
                wait("VE", tok("SE"))
                inc("VE", V.tensor_copy(dbgT[:, 0:32] if ntiles > NCH else dbgT[:, 0:16],
                                        absg[:, 0:ntiles] if ntiles > NCH else absg[:, 0:16]))
                if TERN_PART < 2:
                    return
                V.memset(absgb[:, :], 0.0)
                V.tensor_copy(absgb[:, 0:ntiles], absg[:, 0:ntiles])
                V.tensor_copy(hcl[:, 0:ntiles], absgb[:, 0:ntiles])
                V.tensor_tensor(hcl[:, 0:ntiles], absg[:, 0:ntiles],
                                hcl[:, 0:ntiles], ALU.subtract)
                inc("VE", V.tensor_copy(absgb[:, 128:128 + ntiles],
                                        hcl[:, 0:ntiles]))
                wait("TE", tok("VE"))
                inc("TE", T.matmul(ps_w[:, 0:256], onesb[:, :], absgb[:, 0:256],
                                   start=True, stop=True))
                wait("VE", tok("TE"))
                V.reduce_sum(hcl[:, 0:1], ps_w[:, 0:256], axis=AX.X)
                V.drain()
                inc("VE", V.tensor_scalar(ws_col[:, :], hcl[:, 0:1],
                                          1.0 / (ntiles * 128 * ncols), EPS_FFN,
                                          ALU.mult, ALU.max))
                inc("VE", V.tensor_copy(dbgT[:, 32:33], ws_col[:, :]))
                if TERN_PART < 3:
                    return
                if use_sign:
                    V.tensor_scalar(hcol[:, :], ws_col[:, :], -0.5, 0.0,
                                    ALU.mult, ALU.add)
                    inc("VE", V.tensor_scalar(nhcol[:, :], ws_col[:, :], 0.5, 0.0,
                                              ALU.mult, ALU.add))
                else:
                    inc("VE", V.reciprocal(hcol[:, :], ws_col[:, :]))
                freed = [None, None]
                st_tok = None
                for r in range(ntiles):
                    bi = r % 2
                    wait("LD", freed[bi])
                    lt = inc("LD", SY.dma_start(bufs[bi][:, :],
                                                wT_ap[r * 128:(r + 1) * 128, :]))
                    if use_sign:
                        wait("SE", lt, tok("VE"))
                        SC.sign(tern[:, :], bufs[bi][:, :], bias=hcol[:, :])
                        freed[bi] = inc("SE", SC.sign(scrap[:, :], bufs[bi][:, :],
                                                      bias=nhcol[:, :]))
                        wait("VE", freed[bi], st_tok)
                        vt = inc("VE", V.tensor_tensor(terns[:, :], tern[:, :],
                                                       scrap[:, :], ALU.add))
                    else:
                        wait("VE", lt, st_tok)
                        V.tensor_scalar_mul(tmpf[:, :], bufs[bi][:, :], hcol[:, :])
                        freed[bi] = tok("VE")
                        V.tensor_copy(i32s[:, :], tmpf[:, :])
                        V.tensor_copy(tern[:, :], i32s[:, :])
                        vt = inc("VE", V.tensor_scalar(terns[:, :], tern[:, :],
                                                       1.0, -1.0, ALU.min, ALU.max))
                        freed[bi] = vt
                    wait("GP", vt)
                    st_tok = inc("GP", GP.dma_start(dst_hbm[r, :, :], terns[:, :]))
                barrier()

        if STAGES >= 5 and DO_TERN:
            ternarize(wupT, wupt_hbm, NCH, FI, wsu_col, use_sign=True)
            ternarize(wdnT, wdnt_hbm, NCH2, D, wsd_col, use_sign=True)
        barrier()

        s23 = ExitStack()
        KT_sb = s23.enter_context(nc.sbuf_tensor(_nm("KT_sb"), [128, KVH * S], BF16))
        Vaug = s23.enter_context(nc.sbuf_tensor(_nm("Vaug"), [128, KVH * NT * 130], BF16))
        xc = s23.enter_context(nc.sbuf_tensor(_nm("xc"), [128, D], BF16))
        diag = s23.enter_context(nc.sbuf_tensor(_nm("diag"), [128, 128], BF16))
        bn6 = s23.enter_context(nc.sbuf_tensor(_nm("bn6"), [128, 24], F32))
        mv = s23.enter_context(nc.sbuf_tensor(_nm("mv"), [128, 2], F32))
        rta = s23.enter_context(nc.sbuf_tensor(_nm("rta"), [128, KVH * 64], BF16))
        rtb = s23.enter_context(nc.sbuf_tensor(_nm("rtb"), [128, KVH * 64], BF16))
        Vaug4 = Vaug[:, :].rearrange("p (kv t c) -> p kv t c", kv=KVH, t=NT)
        KT4 = KT_sb[:, :].rearrange("p (kv s) -> p kv s", kv=KVH)

        with s23:
            s2 = ExitStack()
            cosb = s2.enter_context(nc.sbuf_tensor(_nm("cosb"), [128, NT * 64], BF16))
            sinb = s2.enter_context(nc.sbuf_tensor(_nm("sinb"), [128, NT * 64], BF16))
            ctf = s2.enter_context(nc.sbuf_tensor(_nm("ctf"), [128, NT * 64], F32))
            wk_sb = s2.enter_context(nc.sbuf_tensor(_nm("wk_sb"), [128, NCH * KVD], BF16))
            wv_sb = s2.enter_context(nc.sbuf_tensor(_nm("wv_sb"), [128, NCH * KVD], BF16))
            xld = [s2.enter_context(nc.sbuf_tensor(_nm(f"x_{i}"), [128, D], F32))
                   for i in range(2)]
            xn_sb = [s2.enter_context(nc.sbuf_tensor(_nm(f"xn_{i}"), [128, D], BF16))
                     for i in range(2)]
            kt_f = s2.enter_context(nc.sbuf_tensor(_nm("kt_f"), [128, KVD], F32))
            sqv = s2.enter_context(nc.sbuf_tensor(_nm("sqv"), [128, KVD], F32))
            ksum = s2.enter_context(nc.sbuf_tensor(_nm("ksum"), [128, KVH], F32))
            krs = s2.enter_context(nc.sbuf_tensor(_nm("krs"), [128, KVH], F32))
            kt_b = s2.enter_context(nc.sbuf_tensor(_nm("kt_b"), [128, KVD], BF16))
            krt = s2.enter_context(nc.sbuf_tensor(_nm("krt"), [128, KVD], BF16))
            ps_ln = s2.enter_context(nc.psum_tensor(_nm("ps_ln"), [128, D], F32))
            ps_k = s2.enter_context(nc.psum_tensor(_nm("ps_k"), [128, KVD], F32))
            ps_v = s2.enter_context(nc.psum_tensor(_nm("ps_v"), [128, KVD], F32))
            ps_tr = s2.enter_context(nc.psum_tensor(_nm("ps_tr"), [128, KVD], BF16))
            with s2:
              if DO_S2:
                  l1 = inc("LD", SY.dma_start(
                      ctf[:, :].rearrange("p (t f) -> p t f", t=NT),
                      cos_all.rearrange("t p f -> p t f")))
                  wait("VE", l1)
                  c1 = inc("VE", V.tensor_copy(cosb[:, :], ctf[:, :]))
                  wait("LD", c1)
                  l2 = inc("LD", SY.dma_start(
                      ctf[:, :].rearrange("p (t f) -> p t f", t=NT),
                      sin_all.rearrange("t p f -> p t f")))
                  wait("VE", l2)
                  cos_tok = inc("VE", V.tensor_copy(sinb[:, :], ctf[:, :]))
                  inc("GP", GP.dma_start(
                      wk_sb[:, :].rearrange("p (c e) -> p c e", c=NCH),
                      wkT.rearrange("(c p) e -> p c e", p=128)))
                  wload = inc("GP", GP.dma_start(
                      wv_sb[:, :].rearrange("p (c e) -> p c e", c=NCH),
                      wvT.rearrange("(c p) e -> p c e", p=128)))
                  inc("VE", V.memset(Vaug4[:, :, :, 128:129], 1.0))

                  ln_evac = None
                  xn_free = [None, None]
                  x_free = [None, None]
                  xc_free = None
                  kv_evac = None
                  prep_done = None
                  tr_done = None
                  for t in range(NT):
                      bi = t % 2
                      wait("LD", x_free[bi])
                      lx = inc("LD", SY.dma_start(xld[bi][:, :],
                                                  x_tok[t * 128:(t + 1) * 128, :]))
                      wait("VE", lx)
                      for gi in range(4):
                          V.bn_stats(bn6[:, gi * 6:(gi + 1) * 6],
                                     xld[bi][:, gi * 512:(gi + 1) * 512])
                      V.bn_aggr(mv[:, :], bn6[:, :])
                      V.tensor_scalar(tinyf[:, 0:1], mv[:, 1:2], 1.0, EPS_LN,
                                      ALU.mult, ALU.add)
                      vt = inc("VE", V.reciprocal(tinyf[:, 1:2], tinyf[:, 0:1]))
                      wait("SE", vt)
                      st = inc("SE", SC.activation(tinyf[:, 2:3], tinyf[:, 1:2], AF.Sqrt))
                      wait("VE", st, xc_free)
                      V.tensor_scalar(xc[:, :], xld[bi][:, :], mv[:, 0:1], None,
                                      ALU.subtract)
                      x_free[bi] = cd = inc("VE", V.tensor_scalar_mul(
                          diag[:, :], identb[:, :], tinyf[:, 2:3]))
                      wait("TE", cd, ln_evac)
                      for c in range(NCH):
                          mm = T.matmul(ps_ln[:, c * 128:(c + 1) * 128],
                                        xc[:, c * 128:(c + 1) * 128], diag[:, :],
                                        start=True, stop=True)
                      xc_free = dg = inc("TE", mm)
                      wait("SE", dg, xn_free[bi])
                      ln_evac = inc("SE", SC.copy(xn_sb[bi][:, :], ps_ln[:, :]))
                      if t == 0:
                          wait("TE", wload)
                      wait("TE", ln_evac, kv_evac)
                      for c in range(NCH):
                          T.matmul(ps_k[:, :], xn_sb[bi][:, c * 128:(c + 1) * 128],
                                   wk_sb[:, c * KVD:(c + 1) * KVD],
                                   start=(c == 0), stop=False)
                          T.matmul(ps_v[:, :], xn_sb[bi][:, c * 128:(c + 1) * 128],
                                   wv_sb[:, c * KVD:(c + 1) * KVD],
                                   start=(c == 0), stop=False)
                      T.matmul(ps_k[:, :], onesb[:, :], bkv_pad[:, 0:KVD],
                               start=False, stop=True)
                      xn_free[bi] = kv_done = inc("TE", T.matmul(
                          ps_v[:, :], onesb[:, :], bkv_pad[:, KVD:2 * KVD],
                          start=False, stop=True))
                      wait("SE", kv_done, prep_done)
                      inc("SE", SC.copy(
                          Vaug4[:, :, t, 0:128],
                          ps_v[:, :].rearrange("p (kv c) -> p kv c", kv=KVH)))
                      kv_evac = se_k = inc("SE", SC.copy(kt_f[:, :], ps_k[:, :]))
                      wait("VE", se_k, cos_tok)
                      V.tensor_mul(sqv[:, :], kt_f[:, :], kt_f[:, :])
                      V.reduce_sum(ksum[:, :],
                                   sqv[:, :].rearrange("p (kv c) -> p kv c", kv=KVH),
                                   axis=AX.X)
                      V.tensor_scalar(tinyf[:, 4:8], ksum[:, :], 1.0 / HD, EPS_RMS,
                                      ALU.mult, ALU.add)
                      vr = inc("VE", V.reciprocal(tinyf[:, 8:12], tinyf[:, 4:8]))
                      wait("SE", vr)
                      sq = inc("SE", SC.activation(krs[:, :], tinyf[:, 8:12], AF.Sqrt))
                      wait("VE", sq)
                      kb4 = kt_b[:, :].rearrange("p (kv c) -> p kv c", kv=KVH)
                      V.tensor_tensor(
                          kb4, kt_f[:, :].rearrange("p (kv c) -> p kv c", kv=KVH),
                          krs[:, :].rearrange("p (kv o) -> p kv o", o=1).broadcast_to([128, KVH, HD]),
                          ALU.mult)
                      kr4 = krt[:, :].rearrange("p (kv c) -> p kv c", kv=KVH)
                      cosr = cosb[:, t * 64:(t + 1) * 64].rearrange(
                          "p (o f) -> p o f", o=1).broadcast_to([128, KVH, 64])
                      sinr = sinb[:, t * 64:(t + 1) * 64].rearrange(
                          "p (o f) -> p o f", o=1).broadcast_to([128, KVH, 64])
                      rta4 = rta[:, :].rearrange("p (kv f) -> p kv f", kv=KVH)
                      rtb4 = rtb[:, :].rearrange("p (kv f) -> p kv f", kv=KVH)
                      V.tensor_tensor(rta4, kb4[:, :, 0:64], cosr, ALU.mult)
                      V.tensor_tensor(rtb4, kb4[:, :, 64:128], sinr, ALU.mult)
                      V.tensor_tensor(kr4[:, :, 0:64], rta4, rtb4, ALU.add)
                      V.tensor_tensor(rta4, kb4[:, :, 64:128], cosr, ALU.mult)
                      V.tensor_tensor(rtb4, kb4[:, :, 0:64], sinr, ALU.mult)
                      prep_done = inc("VE", V.tensor_tensor(
                          kr4[:, :, 64:128], rta4, rtb4, ALU.subtract))
                      wait("TE", prep_done, tr_done)
                      for kv in range(KVH):
                          tt = T.transpose(ps_tr[:, kv * 128:(kv + 1) * 128],
                                           krt[:, kv * 128:(kv + 1) * 128], identb[:, :])
                      ktr = inc("TE", tt)
                      wait("VE", ktr)
                      tr_done = inc("VE", V.tensor_copy(
                          KT4[:, :, t * 128:(t + 1) * 128],
                          ps_tr[:, :].rearrange("p (kv c) -> p kv c", kv=KVH)))
            barrier()

            if STAGES >= 3 and DO_S3:
                s3 = ExitStack()
                coso = s3.enter_context(nc.sbuf_tensor(_nm("coso"), [128, OT * 64], BF16))
                sino = s3.enter_context(nc.sbuf_tensor(_nm("sino"), [128, OT * 64], BF16))
                ctf3 = s3.enter_context(nc.sbuf_tensor(_nm("ctf3"), [128, OT * 64], F32))
                xoa = s3.enter_context(nc.sbuf_tensor(_nm("xoa"), [128, D], F32))
                xnq = s3.enter_context(nc.sbuf_tensor(_nm("xnq"), [128, 4 * D], BF16))
                QT_sb = s3.enter_context(nc.sbuf_tensor(_nm("QT_sb"), [128, H * 512], BF16))
                OT_sb = s3.enter_context(nc.sbuf_tensor(_nm("OT_sb"), [128, H * 512], BF16))
                wbuf = s3.enter_context(nc.sbuf_tensor(_nm("wbuf"), [128, NCH * 1024], BF16))
                qt_f = s3.enter_context(nc.sbuf_tensor(_nm("qt_f"), [128, 512], F32))
                qsq = s3.enter_context(nc.sbuf_tensor(_nm("qsq"), [128, 512], F32))
                qt_b = s3.enter_context(nc.sbuf_tensor(_nm("qt_b"), [128, 512], BF16))
                qrt = s3.enter_context(nc.sbuf_tensor(_nm("qrt"), [128, 512], BF16))
                o_tok = s3.enter_context(nc.sbuf_tensor(_nm("o_tok"), [128, 128], BF16))
                rcol = s3.enter_context(nc.sbuf_tensor(_nm("rcol"), [128, 1], F32))
                rcol4 = [s3.enter_context(nc.sbuf_tensor(_nm(f"rcol4_{i}"),
                                                         [128, 1], F32))
                         for i in range(4)]
                PT = [s3.enter_context(nc.sbuf_tensor(_nm(f"PT{i}"), [128, 512], BF16))
                      for i in range(2)]
                xres = s3.enter_context(nc.sbuf_tensor(_nm("xres"), [128, 1024], F32))
                outp = s3.enter_context(nc.sbuf_tensor(_nm("outp"), [128, 1024], F32))
                QT16 = QT_sb[:, :].rearrange("p (h s) -> p h s", h=H)
                with s3:
                    l1 = inc("LD", SY.dma_start(
                        ctf3[:, :].rearrange("p (t f) -> p t f", t=OT),
                        cos_own.rearrange("t p f -> p t f")))
                    wait("VE", l1)
                    c1 = inc("VE", V.tensor_copy(coso[:, :], ctf3[:, :]))
                    wait("LD", c1)
                    l2 = inc("LD", SY.dma_start(
                        ctf3[:, :].rearrange("p (t f) -> p t f", t=OT),
                        sin_own.rearrange("t p f -> p t f")))
                    wait("VE", l2)
                    coso_tok = inc("VE", V.tensor_copy(sino[:, :], ctf3[:, :]))
                    for qc in range(4):
                        _s3_qc(qc, nc, locals())
            barrier()
        barrier()

        if STAGES >= 5 and DO_S5:
            _s5_ffn(nc, locals())

        if DBG:
            with nc.sbuf_tensor(_nm("dbg_g"), [128, 80], F32) as dbg_g:
                barrier()
                V.memset(dbg_g[:, :], 0.0)
                V.tensor_copy(dbg_g[:, 0:OT], s_own[:, :])
                V.tensor_copy(dbg_g[:, 16:16 + OT], r1_grid[:, :])
                V.tensor_copy(dbg_g[:, 32:32 + OT], fs_grid[:, :])
                V.tensor_copy(dbg_g[:, 48:49], wsu_col[:, :])
                V.tensor_copy(dbg_g[:, 49:50], wsd_col[:, :])
                V.tensor_copy(dbg_g[:, 50:50 + H], bgrid[:, :])
                V.tensor_copy(dbg_g[:, 66:66 + 14], bshadow[:, 0:14])
                dbg_g2 = None
                inc("VE", V.engine_nop())
                wait("LD", tok("VE"))
                inc("LD", SY.dma_start(dbg_grids_ap, dbg_g[:, :]))
                inc("LD", SY.dma_start(dbg_tern_ap, dbgT[:, :]))
        barrier(["LD", "GP"])

    return nc


def _s3_qc(qc, nc, env):
    (inc, tok, wait, V, SC, T, SY, GP) = (env["inc"], env["tok"], env["wait"],
                                          env["V"], env["SC"], env["T"],
                                          env["SY"], env["GP"])
    identb, identf, onesb = env["identb"], env["identf"], env["onesb"]
    maskes, maskos = env["maskes"], env["maskos"]
    bgrid, qgg, s_own = env["bgrid"], env["qgg"], env["s_own"]
    bq_pad, bo_pad, tinyf = env["bq_pad"], env["bo_pad"], env["tinyf"]
    x_own, wqT, woT = env["x_own"], env["wqT"], env["woT"]
    partial_hbm = env["partial_hbm"]
    KT4, Vaug4 = env["KT4"], env["Vaug4"]
    xc, diag, bn6, mv, rta, rtb = (env["xc"], env["diag"], env["bn6"], env["mv"],
                                   env["rta"], env["rtb"])
    coso, sino, xoa, xnq = env["coso"], env["sino"], env["xoa"], env["xnq"]
    QT_sb, OT_sb, wbuf = env["QT_sb"], env["OT_sb"], env["wbuf"]
    qt_f, qsq, qt_b, qrt = env["qt_f"], env["qsq"], env["qt_b"], env["qrt"]
    o_tok, rcol, PT = env["o_tok"], env["rcol"], env["PT"]
    rcol4 = env["rcol4"]
    xres, outp = env["xres"], env["outp"]
    QT16, coso_tok = env["QT16"], env["coso_tok"]

    pa = ExitStack()
    ps_xn = pa.enter_context(nc.psum_tensor(_nm("ps_xn"), [128, D], F32))
    ps_q = [pa.enter_context(nc.psum_tensor(_nm(f"ps_q{i}"), [128, 512], F32))
            for i in range(2)]
    ps_qtr = pa.enter_context(nc.psum_tensor(_nm("ps_qtr"), [128, 512], BF16))
    with pa:
        xn_ev = None
        xoa_free = tok("VE")
        for lt in range(4):
            i = qc * 4 + lt
            wait("LD", xoa_free)
            lx = inc("LD", SY.dma_start(xoa[:, :], x_own[i, :, :]))
            wait("VE", lx, tok("TE"))
            for gi in range(4):
                V.bn_stats(bn6[:, gi * 6:(gi + 1) * 6],
                           xoa[:, gi * 512:(gi + 1) * 512])
            V.bn_aggr(mv[:, :], bn6[:, :])
            V.drain()
            V.tensor_scalar(tinyf[:, 0:1], mv[:, 1:2], 1.0, EPS_LN,
                            ALU.mult, ALU.add)
            vt = inc("VE", V.reciprocal(tinyf[:, 1:2], tinyf[:, 0:1]))
            wait("SE", vt)
            st = inc("SE", SC.activation(s_own[:, i:i + 1], tinyf[:, 1:2], AF.Sqrt))
            wait("VE", st)
            V.tensor_scalar(xc[:, :], xoa[:, :], mv[:, 0:1], None, ALU.subtract)
            xoa_free = cd = inc("VE", V.tensor_scalar_mul(
                diag[:, :], identb[:, :], s_own[:, i:i + 1]))
            wait("TE", cd, xn_ev)
            for c in range(NCH):
                mm = T.matmul(ps_xn[:, c * 128:(c + 1) * 128],
                              xc[:, c * 128:(c + 1) * 128], diag[:, :],
                              start=True, stop=True)
            dg = inc("TE", mm)
            wait("SE", dg)
            xn_ev = inc("SE", SC.copy(xnq[:, lt * D:(lt + 1) * D], ps_xn[:, :]))
        for kvh in range(KVH):
            wait("GP", tok("TE"))
            wl = inc("GP", GP.dma_start(
                wbuf[:, 0:NCH * 512].rearrange("p (c e) -> p c e", c=NCH),
                wqT.rearrange("(c p) e -> p c e", p=128)[:, :, kvh * 512:(kvh + 1) * 512]))
            qtr_ev = None
            for lt in range(4):
                i = qc * 4 + lt
                pq = ps_q[lt % 2]
                wait("TE", wl, xn_ev, tok("SE"))
                for c in range(NCH):
                    T.matmul(pq[:, :],
                             xnq[:, lt * D + c * 128:lt * D + (c + 1) * 128],
                             wbuf[:, c * 512:(c + 1) * 512],
                             start=(c == 0), stop=False)
                qm = inc("TE", T.matmul(pq[:, :], onesb[:, :],
                                        bq_pad[:, kvh * 512:(kvh + 1) * 512],
                                        start=False, stop=True))
                wait("SE", qm, tok("VE"))
                qe = inc("SE", SC.copy(qt_f[:, :], pq[:, :]))
                wait("VE", qe, coso_tok)
                V.tensor_mul(qsq[:, :], qt_f[:, :], qt_f[:, :])
                V.reduce_sum(tinyf[:, 4:8],
                             qsq[:, :].rearrange("p (h c) -> p h c", h=4),
                             axis=AX.X)
                V.drain()
                V.tensor_scalar(tinyf[:, 4:8], tinyf[:, 4:8], 1.0 / HD, EPS_RMS,
                                ALU.mult, ALU.add)
                vr = inc("VE", V.reciprocal(tinyf[:, 8:12], tinyf[:, 4:8]))
                wait("SE", vr)
                sq = inc("SE", SC.activation(tinyf[:, 12:16], tinyf[:, 8:12], AF.Sqrt))
                wait("VE", sq)
                V.tensor_tensor(tinyf[:, 4:8], tinyf[:, 12:16],
                                qgg[:, kvh * 4:(kvh + 1) * 4], ALU.mult)
                qb4 = qt_b[:, :].rearrange("p (h c) -> p h c", h=4)
                V.tensor_tensor(
                    qb4, qt_f[:, :].rearrange("p (h c) -> p h c", h=4),
                    tinyf[:, 4:8].rearrange("p (h o) -> p h o", o=1).broadcast_to([128, 4, HD]),
                    ALU.mult)
                qr4 = qrt[:, :].rearrange("p (h c) -> p h c", h=4)
                cosr = coso[:, i * 64:(i + 1) * 64].rearrange(
                    "p (o f) -> p o f", o=1).broadcast_to([128, 4, 64])
                sinr = sino[:, i * 64:(i + 1) * 64].rearrange(
                    "p (o f) -> p o f", o=1).broadcast_to([128, 4, 64])
                rta4 = rta[:, :].rearrange("p (h f) -> p h f", h=4)
                rtb4 = rtb[:, :].rearrange("p (h f) -> p h f", h=4)
                V.tensor_tensor(rta4, qb4[:, :, 0:64], cosr, ALU.mult)
                V.tensor_tensor(rtb4, qb4[:, :, 64:128], sinr, ALU.mult)
                V.tensor_tensor(qr4[:, :, 0:64], rta4, rtb4, ALU.add)
                V.tensor_tensor(rta4, qb4[:, :, 64:128], cosr, ALU.mult)
                V.tensor_tensor(rtb4, qb4[:, :, 0:64], sinr, ALU.mult)
                rp = inc("VE", V.tensor_tensor(qr4[:, :, 64:128], rta4, rtb4,
                                               ALU.subtract))
                wait("TE", rp, qtr_ev)
                for hl in range(4):
                    tq = T.transpose(ps_qtr[:, hl * 128:(hl + 1) * 128],
                                     qrt[:, hl * 128:(hl + 1) * 128], identb[:, :])
                tq = inc("TE", tq)
                wait("VE", tq)
                qtr_ev = inc("VE", V.tensor_copy(
                    QT16[:, kvh * 4:(kvh + 1) * 4, lt * 128:(lt + 1) * 128],
                    ps_qtr[:, :].rearrange("p (h c) -> p h c", h=4)))

    pb = ExitStack()
    ps_st = [pb.enter_context(nc.psum_tensor(_nm(f"ps_st{i}"), [128, 512], F32))
             for i in range(2)]
    ps_o = [pb.enter_context(nc.psum_tensor(_nm(f"ps_o{i}"), [128, 130], F32))
            for i in range(4)]
    ps_otr = pb.enter_context(nc.psum_tensor(_nm("ps_otr"), [128, 128], BF16))
    with pb:
        wait("TE", tok("VE"), tok("SE"))
        ktmax = 8 * qc + 8
        for kvh in range(KVH):
            for hl in range(4):
                h = kvh * 4 + hl
                pt_free = [tok("TE")] * 2
                st_free = [tok("SE")] * 2
                for kt in range(ktmax):
                    db = kt % 2
                    ist = max(kt // 2 - 4 * qc, 0)
                    off = ist * 128
                    ib = kt // 2 - 4 * qc
                    wait("TE", st_free[db])
                    mmq = T.matmul(ps_st[db][:, off:512],
                                   KT4[:, kvh, kt * 128:(kt + 1) * 128],
                                   QT_sb[:, h * 512 + off:(h + 1) * 512],
                                   start=True, stop=(ib < 0))
                    if ib >= 0:
                        mk = maskes if kt % 2 == 0 else maskos
                        mmq = T.matmul(ps_st[db][:, off:off + 128],
                                       identb[:, :], mk[:, :],
                                       start=False, stop=True)
                    qk = inc("TE", mmq)
                    wait("SE", qk, pt_free[db])
                    ex = inc("SE", SC.activation(PT[db][:, off:512],
                                                 ps_st[db][:, off:512],
                                                 AF.Exp, bias=bgrid[:, h:h + 1],
                                                 scale=1.0))
                    st_free[db] = ex
                    wait("TE", ex)
                    for ii in range(ist, 4):
                        mm = T.matmul(ps_o[ii][:, 0:129],
                                      PT[db][:, ii * 128:(ii + 1) * 128],
                                      Vaug4[:, kvh, kt, 0:129],
                                      start=(kt == 0),
                                      stop=(kt == 8 * qc + 2 * ii + 1))
                    pv = inc("TE", mm)
                    pt_free[db] = pv
                wait("VE", tok("TE"))
                for ii in range(4):
                    V.reciprocal(rcol4[ii][:, :], ps_o[ii][:, 128:129])
                ot_prev = None
                for ii in range(4):
                    wait("VE", tok("TE"))
                    ev = inc("VE", V.tensor_scalar_mul(o_tok[:, :],
                                                       ps_o[ii][:, 0:128],
                                                       rcol4[ii][:, :]))
                    wait("TE", ev, ot_prev)
                    tr = inc("TE", T.transpose(ps_otr[:, :], o_tok[:, :],
                                               identb[:, :]))
                    wait("SE", tr)
                    ot_prev = inc("SE", SC.copy(
                        OT_sb[:, h * 512 + ii * 128:h * 512 + (ii + 1) * 128],
                        ps_otr[:, :]))
                wait("TE", ot_prev)

    pd = ExitStack()
    ps_wo = [pd.enter_context(nc.psum_tensor(_nm(f"ps_wo{i}"), [128, 512], F32))
             for i in range(2)]
    with pd:
        store_tok = None
        for half in range(2):
            wait("GP", tok("TE"))
            wl = inc("GP", GP.dma_start(
                wbuf[:, :].rearrange("p (c e) -> p c e", c=NCH),
                woT.rearrange("(c p) e -> p c e", p=128)[
                    :, :, half * 1024:(half + 1) * 1024]))
            for lt in range(4):
                i = qc * 4 + lt
                wait("LD", tok("VE"))
                lr = inc("LD", SY.dma_start(
                    xres[:, :], x_own[i, :, half * 1024:(half + 1) * 1024]))
                for sl in range(2):
                    pw = ps_wo[sl]
                    ec = half * 1024 + sl * 512
                    wait("TE", wl, tok("VE"))
                    for c in range(NCH):
                        T.matmul(pw[:, :],
                                 OT_sb[:, c * 512 + lt * 128:c * 512 + (lt + 1) * 128],
                                 wbuf[:, c * 1024 + sl * 512:c * 1024 + (sl + 1) * 512],
                                 start=(c == 0), stop=False)
                    wm = inc("TE", T.matmul(pw[:, :], onesb[:, :],
                                            bo_pad[:, ec:ec + 512],
                                            start=False, stop=True))
                    wait("VE", wm, lr, store_tok)
                    vt = inc("VE", V.tensor_tensor(
                        outp[:, sl * 512:(sl + 1) * 512], pw[:, :],
                        xres[:, sl * 512:(sl + 1) * 512], ALU.add))
                wait("LD", vt)
                store_tok = inc("LD", SY.dma_start(
                    partial_hbm[i, :, half * 1024:(half + 1) * 1024], outp[:, :]))
                if env["dbg_partial_ap"] is not None:
                    store_tok = inc("LD", SY.dma_start(
                        env["dbg_partial_ap"][i, :, half * 1024:(half + 1) * 1024],
                        outp[:, :]))


def _s5_ffn(nc, env):
    (inc, tok, wait, barrier) = (env["inc"], env["tok"], env["wait"], env["barrier"])
    V, SC, T, SY, GP = env["V"], env["SC"], env["T"], env["SY"], env["GP"]
    identf, h128, h128n, onestok = (env["identf"], env["h128"], env["h128n"],
                                    env["onestok"])
    identb, onesb = env["identb"], env["onesb"]
    s_own, r1_grid, fs_grid = env["s_own"], env["r1_grid"], env["fs_grid"]
    wsu_col, wsd_col, tinyf = env["wsu_col"], env["wsd_col"], env["tinyf"]
    xT_own = env["xT_own"]
    partial_hbm, out_hbm = env["partial_hbm"], env["out_hbm"]
    ghi_hbm, glo_hbm = env["ghi_hbm"], env["glo_hbm"]
    wupt_hbm, wdnt_hbm = env["wupt_hbm"], env["wdnt_hbm"]

    H4 = np.array([[1, 1, 1, 1], [1, -1, 1, -1], [1, 1, -1, -1], [1, -1, -1, 1]])

    sa = ExitStack()
    xsl = sa.enter_context(nc.sbuf_tensor(_nm("xsl"), [128, NCH * 256], F32))
    xhi = sa.enter_context(nc.sbuf_tensor(_nm("xhi"), [128, NCH * 256], BF16))
    xlo = sa.enter_context(nc.sbuf_tensor(_nm("xlo"), [128, NCH * 256], BF16))
    xfA = sa.enter_context(nc.sbuf_tensor(_nm("xfA"), [128, NCH * 256], F32))
    xfB = sa.enter_context(nc.sbuf_tensor(_nm("xfB"), [128, NCH * 256], F32))
    i32a = sa.enter_context(nc.sbuf_tensor(_nm("i32a"), [128, NCH * 256], I32))
    xq1 = sa.enter_context(nc.sbuf_tensor(_nm("xq1"), [128, NCH * 512], BF16))
    mm1 = sa.enter_context(nc.sbuf_tensor(_nm("mm1"), [128, 256], F32))
    mmh = sa.enter_context(nc.sbuf_tensor(_nm("mmh"), [128, 256], BF16))
    mml = sa.enter_context(nc.sbuf_tensor(_nm("mml"), [128, 256], BF16))
    scf = sa.enter_context(nc.sbuf_tensor(_nm("scf"), [128, 256], F32))
    tinyb = sa.enter_context(nc.sbuf_tensor(_nm("tinyb"), [128, 2], BF16))
    bth = sa.enter_context(nc.sbuf_tensor(_nm("bth"), [128, 256], BF16))
    btl = sa.enter_context(nc.sbuf_tensor(_nm("btl"), [128, 256], BF16))
    wup_b = [sa.enter_context(nc.sbuf_tensor(_nm(f"wup_{i}"), [128, NCH * 128], BF16))
             for i in range(2)]
    grel = sa.enter_context(nc.sbuf_tensor(_nm("grel"), [128, 512], F32))
    g2 = sa.enter_context(nc.sbuf_tensor(_nm("g2"), [128, 512], F32))
    ghi_sb = sa.enter_context(nc.sbuf_tensor(_nm("ghi_sb"), [128, NCH2 * 512], BF16))
    glo_sb = sa.enter_context(nc.sbuf_tensor(_nm("glo_sb"), [128, NCH2 * 512], BF16))
    ps_f1 = [sa.enter_context(nc.psum_tensor(_nm(f"ps_f1{i}"), [128, 256], F32))
             for i in range(3)]
    ps_up = [sa.enter_context(nc.psum_tensor(_nm(f"ps_up{i}"), [128, 512], F32))
             for i in range(2)]
    ps_txb = sa.enter_context(nc.psum_tensor(_nm("ps_txb"), [128, 512], BF16))
    with sa:
        wt_tok = None
        g_store = None
        for grp in range(4):
            for sl2 in range(2):
                s5 = grp * 2 + sl2
                wait("LD", tok("VE"))
                lx = inc("LD", SY.dma_start(
                    xsl[:, :].rearrange("p (c t) -> p c t", c=NCH),
                    xT_own.rearrange("(c p) t -> p c t", p=128)[
                        :, :, s5 * 256:(s5 + 1) * 256]))
                wait("VE", lx, tok("TE"))
                V.tensor_copy(xhi[:, :], xsl[:, :])
                hl = inc("VE", V.tensor_tensor(xlo[:, :], xsl[:, :], xhi[:, :],
                                               ALU.subtract))
                wait("TE", hl)
                f1_free = [None, None, None]
                for chi in range(4):
                    for co in range(4):
                        pidx = (chi * 4 + co) % 3
                        ps = ps_f1[pidx]
                        wait("TE", f1_free[pidx])
                        order = sorted(range(4), key=lambda ci: -H4[co][ci])
                        first = True
                        for ci in order:
                            lhs = h128 if H4[co][ci] > 0 else h128n
                            cin = chi * 4 + ci
                            T.matmul(ps[:, :], lhs[:, :],
                                     xhi[:, cin * 256:(cin + 1) * 256],
                                     start=first, stop=False)
                            first = False
                            mm = T.matmul(ps[:, :], lhs[:, :],
                                          xlo[:, cin * 256:(cin + 1) * 256],
                                          start=False, stop=(ci == order[-1]))
                        ft = inc("TE", mm)
                        wait("SE", ft)
                        cchunk = chi * 4 + co
                        f1_free[pidx] = inc("SE", SC.copy(
                            xfA[:, cchunk * 256:(cchunk + 1) * 256], ps[:, :]))
                wait("VE", tok("SE"))
                a4 = xfA[:, :].rearrange("p (g c n) -> p g c n", g=2, c=8)
                b4 = xfB[:, :].rearrange("p (g c n) -> p g c n", g=2, c=8)
                a_lo = xfA[:, :].rearrange("p (g c n) -> p g c n", g=4, c=4)
                b_lo = xfB[:, :].rearrange("p (g c n) -> p g c n", g=4, c=4)
                ev_in0 = a_lo[:, 0::2, :, :]
                ev_in1 = a_lo[:, 1::2, :, :]
                V.tensor_tensor(b_lo[:, 0::2, :, :], ev_in0, ev_in1, ALU.add)
                V.tensor_tensor(b_lo[:, 1::2, :, :], ev_in0, ev_in1, ALU.subtract)
                V.tensor_tensor(a4[:, 0, :, :], b4[:, 0, :, :], b4[:, 1, :, :],
                                ALU.add)
                V.tensor_tensor(a4[:, 1, :, :], b4[:, 0, :, :], b4[:, 1, :, :],
                                ALU.subtract)
                V.memset(xfA[0:1, 0:256], 0.0)
                vt = inc("VE", V.reduce_max(
                    mm1[:, :],
                    xfA[:, :].rearrange("p (c n) -> p n c", c=NCH),
                    axis=AX.X, apply_absolute_value=True))
                V.drain()
                V.tensor_copy(mmh[:, :], mm1[:, :])
                vt = inc("VE", V.tensor_tensor(mml[:, :], mm1[:, :], mmh[:, :],
                                               ALU.subtract))
                wait("TE", vt)
                T.transpose(ps_txb[:, 0:128], mmh[:, 0:128], identb[:, :])
                T.transpose(ps_txb[:, 128:256], mmh[:, 128:256], identb[:, :])
                T.transpose(ps_txb[:, 256:384], mml[:, 0:128], identb[:, :])
                tt = inc("TE", T.transpose(ps_txb[:, 384:512], mml[:, 128:256],
                                           identb[:, :]))
                wait("VE", tt)
                V.tensor_copy(scf[:, 0:256], ps_txb[:, 0:256])
                V.tensor_tensor(scf[:, 0:256], scf[:, 0:256], ps_txb[:, 256:512],
                                ALU.add)
                cols = slice(s5 * 2, s5 * 2 + 2)
                V.reduce_max(r1_grid[:, cols],
                             scf[:, 0:256].rearrange("p (s n) -> p s n", s=2),
                             axis=AX.X)
                V.drain()
                V.tensor_scalar(tinyf[:, 0:2], r1_grid[:, cols], 1.0, 1e-30,
                                ALU.mult, ALU.max)
                V.reciprocal(tinyf[:, 2:4], tinyf[:, 0:2])
                V.drain()
                vs = inc("VE", V.tensor_scalar(tinyf[:, 4:6], tinyf[:, 2:4], 127.0,
                                               0.0, ALU.mult, ALU.add))
                V.tensor_copy(tinyb[:, 0:2], tinyf[:, 4:6])
                V.tensor_copy(tinyf[:, 6:8], tinyb[:, 0:2])
                vt2 = inc("VE", V.tensor_tensor(tinyf[:, 8:10], tinyf[:, 4:6],
                                                tinyf[:, 6:8], ALU.subtract))
                sc_tr = None
                for seg in range(2):
                    wait("VE", sc_tr, vt2)
                    V.tensor_scalar_mul(bth[:, seg * 128:(seg + 1) * 128],
                                        onesb[:, :], tinyf[:, 6 + seg:7 + seg])
                    bt = inc("VE", V.tensor_scalar_mul(
                        btl[:, seg * 128:(seg + 1) * 128], onesb[:, :],
                        tinyf[:, 8 + seg:9 + seg]))
                    wait("TE", bt)
                    T.transpose(ps_txb[:, seg * 128:(seg + 1) * 128],
                                bth[:, seg * 128:(seg + 1) * 128], identb[:, :])
                    sc_tr = inc("TE", T.transpose(
                        ps_txb[:, 256 + seg * 128:256 + (seg + 1) * 128],
                        btl[:, seg * 128:(seg + 1) * 128], identb[:, :]))
                wait("VE", sc_tr)
                V.tensor_copy(scf[:, 0:256], ps_txb[:, 0:256])
                V.tensor_tensor(scf[:, 0:256], scf[:, 0:256], ps_txb[:, 256:512],
                                ALU.add)
                V.tensor_tensor(
                    xfB[:, :].rearrange("p (c n) -> p c n", c=NCH),
                    xfA[:, :].rearrange("p (c n) -> p c n", c=NCH),
                    scf[:, 0:256].rearrange("p (o n) -> p o n", o=1).broadcast_to([128, NCH, 256]),
                    ALU.mult)
                V.tensor_scalar(xfB[:, :], xfB[:, :], MAGIC, MAGIC,
                                ALU.add, ALU.subtract)
                V.tensor_copy(i32a[:, :], xfB[:, :])
                q1t = inc("VE", V.tensor_copy(
                    xq1[:, :].rearrange("p (c n) -> p c n", c=NCH)[
                        :, :, sl2 * 256:(sl2 + 1) * 256],
                    i32a[:, :].rearrange("p (c n) -> p c n", c=NCH)))
            for fc in range(NCH2):
                wait("GP", wt_tok)
                wl = inc("GP", GP.dma_start(
                    wup_b[fc % 2][:, :].rearrange("p (d c) -> p d c", d=NCH),
                    wupt_hbm[:, :, fc * 128:(fc + 1) * 128].rearrange(
                        "d p c -> p d c")))
                pu = ps_up[fc % 2]
                wait("TE", wl, q1t, tok("SE"))
                for dc in range(NCH):
                    mm = T.matmul(pu[:, :],
                                  wup_b[fc % 2][:, dc * 128:(dc + 1) * 128],
                                  xq1[:, dc * 512:(dc + 1) * 512],
                                  start=(dc == 0), stop=(dc == NCH - 1))
                um = inc("TE", mm)
                wt_tok = um
                wait("SE", um, tok("VE"))
                SC.activation(grel[:, :], pu[:, :], AF.Relu)
                sg = inc("SE", SC.activation(g2[:, :], grel[:, :], AF.Square))
                wait("VE", sg, g_store)
                V.tensor_copy(ghi_sb[:, fc * 512:(fc + 1) * 512], g2[:, :])
                inc("VE", V.tensor_tensor(glo_sb[:, fc * 512:(fc + 1) * 512],
                                          g2[:, :],
                                          ghi_sb[:, fc * 512:(fc + 1) * 512],
                                          ALU.subtract))
            wait("LD", tok("VE"))
            inc("LD", SY.dma_start(ghi_hbm[grp, :, :], ghi_sb[:, :]))
            g_store = inc("LD", SY.dma_start(glo_hbm[grp, :, :], glo_sb[:, :]))
            if env["dbg_ghi_ap"] is not None:
                g_store = inc("LD", SY.dma_start(env["dbg_ghi_ap"][grp, :, :],
                                                 ghi_sb[:, :]))
    barrier()

    sb5 = ExitStack()
    gh_sl = sb5.enter_context(nc.sbuf_tensor(_nm("gh_sl"), [128, NCH2 * 256], BF16))
    gl_sl = sb5.enter_context(nc.sbuf_tensor(_nm("gl_sl"), [128, NCH2 * 256], BF16))
    yfA = sb5.enter_context(nc.sbuf_tensor(_nm("yfA"), [128, NCH2 * 256], F32))
    yfB = sb5.enter_context(nc.sbuf_tensor(_nm("yfB"), [128, NCH2 * 256], F32))
    i32b = sb5.enter_context(nc.sbuf_tensor(_nm("i32b"), [128, NCH2 * 256], I32))
    xq2 = sb5.enter_context(nc.sbuf_tensor(_nm("xq2"), [128, NCH2 * 512], BF16))
    mm2 = sb5.enter_context(nc.sbuf_tensor(_nm("mm2"), [128, 256], F32))
    mmh2 = sb5.enter_context(nc.sbuf_tensor(_nm("mmh2"), [128, 256], BF16))
    mml2 = sb5.enter_context(nc.sbuf_tensor(_nm("mml2"), [128, 256], BF16))
    scf2 = sb5.enter_context(nc.sbuf_tensor(_nm("scf2"), [128, 256], F32))
    tinyb2 = sb5.enter_context(nc.sbuf_tensor(_nm("tinyb2"), [128, 2], BF16))
    bth2 = sb5.enter_context(nc.sbuf_tensor(_nm("bth2"), [128, 256], BF16))
    btl2 = sb5.enter_context(nc.sbuf_tensor(_nm("btl2"), [128, 256], BF16))
    wdn_b = [sb5.enter_context(nc.sbuf_tensor(_nm(f"wdn_{i}"), [128, 1024], BF16))
             for i in range(2)]
    par_sb = sb5.enter_context(nc.sbuf_tensor(_nm("par_sb"), [128, 1024], F32))
    out_sb = sb5.enter_context(nc.sbuf_tensor(_nm("out_sb"), [128, 1024], F32))
    with sb5:
        for grp in range(4):
            f2 = ExitStack()
            ps_f2 = [f2.enter_context(nc.psum_tensor(_nm(f"ps_f2{i}"), [128, 256], F32))
                     for i in range(3)]
            ps_txb2 = f2.enter_context(nc.psum_tensor(_nm("ps_tx2"), [128, 512], BF16))
            with f2:
                for sl2 in range(2):
                    s5 = grp * 2 + sl2
                    wait("LD", tok("VE"), tok("TE"))
                    inc("LD", SY.dma_start(
                        gh_sl[:, :].rearrange("p (c n) -> p c n", c=NCH2),
                        ghi_hbm[grp].rearrange("p (c m) -> p c m", c=NCH2)[
                            :, :, sl2 * 256:(sl2 + 1) * 256]))
                    lg = inc("LD", SY.dma_start(
                        gl_sl[:, :].rearrange("p (c n) -> p c n", c=NCH2),
                        glo_hbm[grp].rearrange("p (c m) -> p c m", c=NCH2)[
                            :, :, sl2 * 256:(sl2 + 1) * 256]))
                    wait("TE", lg)
                    f2_free = [None, None, None]
                    for chi in range(8):
                        for co in range(4):
                            pidx = (chi * 4 + co) % 3
                            ps = ps_f2[pidx]
                            wait("TE", f2_free[pidx])
                            order = sorted(range(4), key=lambda ci: -H4[co][ci])
                            first = True
                            for ci in order:
                                lhs = h128 if H4[co][ci] > 0 else h128n
                                cin = chi * 4 + ci
                                T.matmul(ps[:, :], lhs[:, :],
                                         gh_sl[:, cin * 256:(cin + 1) * 256],
                                         start=first, stop=False)
                                first = False
                                mm = T.matmul(ps[:, :], lhs[:, :],
                                              gl_sl[:, cin * 256:(cin + 1) * 256],
                                              start=False, stop=(ci == order[-1]))
                            ft = inc("TE", mm)
                            wait("SE", ft)
                            cchunk = chi * 4 + co
                            f2_free[pidx] = inc("SE", SC.copy(
                                yfA[:, cchunk * 256:(cchunk + 1) * 256], ps[:, :]))
                    wait("VE", tok("SE"))
                    a_s4 = yfA[:, :].rearrange("p (g c n) -> p g c n", g=4, c=8)
                    b_s4 = yfB[:, :].rearrange("p (g c n) -> p g c n", g=4, c=8)
                    a4i0 = yfA[:, :].rearrange("p (g c n) -> p g c n", g=8, c=4)
                    b4i0 = yfB[:, :].rearrange("p (g c n) -> p g c n", g=8, c=4)
                    V.tensor_tensor(b4i0[:, 0::2, :, :], a4i0[:, 0::2, :, :],
                                    a4i0[:, 1::2, :, :], ALU.add)
                    V.tensor_tensor(b4i0[:, 1::2, :, :], a4i0[:, 0::2, :, :],
                                    a4i0[:, 1::2, :, :], ALU.subtract)
                    V.tensor_tensor(a_s4[:, 0::2, :, :], b_s4[:, 0::2, :, :],
                                    b_s4[:, 1::2, :, :], ALU.add)
                    V.tensor_tensor(a_s4[:, 1::2, :, :], b_s4[:, 0::2, :, :],
                                    b_s4[:, 1::2, :, :], ALU.subtract)
                    a16 = yfA[:, :].rearrange("p (g c n) -> p g c n", g=2, c=16)
                    b16 = yfB[:, :].rearrange("p (g c n) -> p g c n", g=2, c=16)
                    V.tensor_tensor(b16[:, 0, :, :], a16[:, 0, :, :],
                                    a16[:, 1, :, :], ALU.add)
                    V.tensor_tensor(b16[:, 1, :, :], a16[:, 0, :, :],
                                    a16[:, 1, :, :], ALU.subtract)
                    vt = inc("VE", V.reduce_max(
                        mm2[:, :],
                        yfB[:, :].rearrange("p (c n) -> p n c", c=NCH2),
                        axis=AX.X, apply_absolute_value=True))
                    V.drain()
                    V.tensor_copy(mmh2[:, :], mm2[:, :])
                    vt = inc("VE", V.tensor_tensor(mml2[:, :], mm2[:, :], mmh2[:, :],
                                                   ALU.subtract))
                    wait("TE", vt)
                    T.transpose(ps_txb2[:, 0:128], mmh2[:, 0:128], identb[:, :])
                    T.transpose(ps_txb2[:, 128:256], mmh2[:, 128:256], identb[:, :])
                    T.transpose(ps_txb2[:, 256:384], mml2[:, 0:128], identb[:, :])
                    tt = inc("TE", T.transpose(ps_txb2[:, 384:512], mml2[:, 128:256],
                                               identb[:, :]))
                    wait("VE", tt)
                    V.tensor_copy(scf2[:, 0:256], ps_txb2[:, 0:256])
                    V.tensor_tensor(scf2[:, 0:256], scf2[:, 0:256],
                                    ps_txb2[:, 256:512], ALU.add)
                    cols = slice(s5 * 2, s5 * 2 + 2)
                    V.reduce_max(tinyf[:, 6:8],
                                 scf2[:, 0:256].rearrange("p (s n) -> p s n", s=2),
                                 axis=AX.X)
                    V.drain()
                    V.tensor_scalar(tinyf[:, 0:2], tinyf[:, 6:8], 1.0, 1e-30,
                                    ALU.mult, ALU.max)
                    V.reciprocal(tinyf[:, 2:4], tinyf[:, 0:2])
                    V.drain()
                    vs = inc("VE", V.tensor_scalar(tinyf[:, 4:6], tinyf[:, 2:4],
                                                   127.0, 0.0, ALU.mult, ALU.add))
                    V.tensor_tensor(tinyf[:, 8:10], s_own[:, cols],
                                    r1_grid[:, cols], ALU.mult)
                    V.tensor_scalar(tinyf[:, 8:10], tinyf[:, 8:10],
                                    1.0 / math.sqrt(D), EPS_FFN, ALU.mult, ALU.max)
                    V.tensor_tensor(tinyf[:, 8:10], tinyf[:, 8:10],
                                    wsu_col[:, :].broadcast_to([128, 2]), ALU.mult)
                    V.tensor_scalar(tinyf[:, 8:10], tinyf[:, 8:10], 1.0 / 127.0,
                                    0.0, ALU.mult, ALU.add)
                    V.tensor_tensor(tinyf[:, 10:12], tinyf[:, 8:10],
                                    tinyf[:, 8:10], ALU.mult)
                    V.tensor_tensor(tinyf[:, 10:12], tinyf[:, 10:12],
                                    tinyf[:, 6:8], ALU.mult)
                    V.tensor_scalar(tinyf[:, 10:12], tinyf[:, 10:12],
                                    1.0 / (4.0 * math.sqrt(FI)), EPS_FFN,
                                    ALU.mult, ALU.max)
                    V.tensor_tensor(tinyf[:, 10:12], tinyf[:, 10:12],
                                    wsd_col[:, :].broadcast_to([128, 2]), ALU.mult)
                    inc("VE", V.tensor_scalar(fs_grid[:, cols], tinyf[:, 10:12],
                                              1.0 / 254.0, 0.0, ALU.mult, ALU.add))
                    V.tensor_copy(tinyb2[:, 0:2], tinyf[:, 4:6])
                    V.tensor_copy(tinyf[:, 12:14], tinyb2[:, 0:2])
                    vt2 = inc("VE", V.tensor_tensor(tinyf[:, 14:16], tinyf[:, 4:6],
                                                    tinyf[:, 12:14], ALU.subtract))
                    sc_tr = None
                    for seg in range(2):
                        wait("VE", sc_tr, vt2)
                        V.tensor_scalar_mul(bth2[:, seg * 128:(seg + 1) * 128],
                                            onesb[:, :], tinyf[:, 12 + seg:13 + seg])
                        bt = inc("VE", V.tensor_scalar_mul(
                            btl2[:, seg * 128:(seg + 1) * 128], onesb[:, :],
                            tinyf[:, 14 + seg:15 + seg]))
                        wait("TE", bt)
                        T.transpose(ps_txb2[:, seg * 128:(seg + 1) * 128],
                                    bth2[:, seg * 128:(seg + 1) * 128], identb[:, :])
                        sc_tr = inc("TE", T.transpose(
                            ps_txb2[:, 256 + seg * 128:256 + (seg + 1) * 128],
                            btl2[:, seg * 128:(seg + 1) * 128], identb[:, :]))
                    wait("VE", sc_tr)
                    V.tensor_copy(scf2[:, 0:256], ps_txb2[:, 0:256])
                    V.tensor_tensor(scf2[:, 0:256], scf2[:, 0:256],
                                    ps_txb2[:, 256:512], ALU.add)
                    V.tensor_tensor(
                        yfA[:, :].rearrange("p (c n) -> p c n", c=NCH2),
                        yfB[:, :].rearrange("p (c n) -> p c n", c=NCH2),
                        scf2[:, 0:256].rearrange("p (o n) -> p o n", o=1).broadcast_to(
                            [128, NCH2, 256]),
                        ALU.mult)
                    V.tensor_scalar(yfA[:, :], yfA[:, :], MAGIC, MAGIC,
                                    ALU.add, ALU.subtract)
                    V.tensor_copy(i32b[:, :], yfA[:, :])
                    q2t = inc("VE", V.tensor_copy(
                        xq2[:, :].rearrange("p (c n) -> p c n", c=NCH2)[
                            :, :, sl2 * 256:(sl2 + 1) * 256],
                        i32b[:, :].rearrange("p (c n) -> p c n", c=NCH2)))
            dn = ExitStack()
            ps_dn = [dn.enter_context(nc.psum_tensor(_nm(f"ps_dn{i}"), [128, 1024], F32))
                     for i in range(4)]
            with dn:
                wd_tok = None
                par_tok = None
                for half in range(2):
                    for fc in range(NCH2):
                        wait("GP", wd_tok)
                        wl = inc("GP", GP.dma_start(
                            wdn_b[fc % 2][:, :],
                            wdnt_hbm[fc, :, half * 1024:(half + 1) * 1024]))
                        wait("TE", wl, q2t, tok("VE"))
                        for lt2 in range(4):
                            for sl in range(2):
                                mm = T.matmul(
                                    ps_dn[lt2][:, sl * 512:(sl + 1) * 512],
                                    xq2[:, fc * 512 + lt2 * 128:fc * 512 + (lt2 + 1) * 128],
                                    wdn_b[fc % 2][:, sl * 512:(sl + 1) * 512],
                                    start=(fc == 0), stop=(fc == NCH2 - 1))
                        wd_tok = inc("TE", mm)
                    for lt2 in range(4):
                        i = grp * 4 + lt2
                        wait("LD", par_tok, tok("VE"))
                        pl = inc("LD", SY.dma_start(
                            par_sb[:, :],
                            partial_hbm[i, :, half * 1024:(half + 1) * 1024]))
                        wait("VE", wd_tok, pl, par_tok)
                        vt = inc("VE", V.scalar_tensor_tensor(
                            out_sb[:, :], ps_dn[lt2][:, :], fs_grid[:, i:i + 1],
                            par_sb[:, :], ALU.mult, ALU.add))
                        wait("LD", vt)
                        par_tok = inc("LD", SY.dma_start(
                            out_hbm[i, :, half * 1024:(half + 1) * 1024],
                            out_sb[:, :]))
                    wait("TE", tok("VE"))


_PROGRAM_CACHE = {}


def _get_program():
    key = STAGES
    if key not in _PROGRAM_CACHE:
        _PROGRAM_CACHE[key] = build_program()
    return _PROGRAM_CACHE[key]


def _hadamard(n):
    h = np.array([[1.0]], dtype=np.float64)
    while h.shape[0] < n:
        h = np.block([[h, h], [h, -h]])
    return h


def _bf(x):
    return np.ascontiguousarray(np.asarray(x).astype(ml_dtypes.bfloat16))


def _f32c(x):
    return np.ascontiguousarray(np.asarray(x, dtype=np.float32))


def make_in_maps(inputs):
    x = np.asarray(inputs["L0_attn_x"], dtype=np.float32)
    Wq = np.asarray(inputs["L0_attn_Wq"], dtype=np.float32)
    bq = np.asarray(inputs["L0_attn_bq"], dtype=np.float32)
    Wk = np.asarray(inputs["L0_attn_Wk"], dtype=np.float32)
    bk = np.asarray(inputs["L0_attn_bk"], dtype=np.float32)
    Wv = np.asarray(inputs["L0_attn_Wv"], dtype=np.float32)
    bv = np.asarray(inputs["L0_attn_bv"], dtype=np.float32)
    Wo = np.asarray(inputs["L0_attn_Wo"], dtype=np.float32)
    bo = np.asarray(inputs["L0_attn_bo"], dtype=np.float32)
    qg = np.asarray(inputs["L0_attn_q_gain"], dtype=np.float32)
    w_up = np.asarray(inputs["L0_ffn_w_up"], dtype=np.float32)
    w_dn = np.asarray(inputs["L0_ffn_w_down"], dtype=np.float32)

    wqT = _f32c(Wq.T)
    wkT = _f32c(Wk.T)
    wvT = _f32c(Wv.T)
    woT = _f32c(Wo.T)
    wupT = _f32c(w_up.T)
    wdnT = _f32c(w_dn.T)
    bkv = _f32c(np.concatenate([bk, bv])[None, :])

    inv_freq = 1.0 / (10000.0 ** (np.arange(0, HD, 2, dtype=np.float64) / HD))
    tpos = np.arange(S, dtype=np.float64)
    fr = np.outer(tpos, inv_freq)
    cosT = np.cos(fr).reshape(NT, 128, 64)
    sinT = np.sin(fr).reshape(NT, 128, 64)

    Hm = _hadamard(128)
    tri = np.zeros((128, 128), dtype=np.float64)
    kk, qq = np.meshgrid(np.arange(128), np.arange(128), indexing="ij")
    tri[kk > qq] = -30000.0

    in_maps = []
    for core in range(8):
        b, p = core // 2, core % 2
        xb = x[b]
        own = np.arange(p, NT, 2)
        x_ownv = np.ascontiguousarray(xb.reshape(NT, 128, D)[own])
        mask_e = tri if p == 0 else np.zeros((128, 128))
        mask_o = np.full((128, 128), -30000.0) if p == 0 else tri
        m = dict(
            x_tok=_f32c(xb),
            xT_own=_f32c(x_ownv.transpose(2, 0, 1).reshape(D, OT * 128)),
            x_own=_f32c(x_ownv),
            wqT=wqT, wkT=wkT, wvT=wvT, woT=woT, wupT=wupT, wdnT=wdnT,
            bq=_f32c(bq[None, :]), bkv=bkv, bo=_f32c(bo[None, :]),
            qgain=_f32c(qg[None, :]),
            cos_all=_f32c(cosT), sin_all=_f32c(sinT),
            cos_own=_f32c(cosT[own]), sin_own=_f32c(sinT[own]),
            h128_in=_bf(Hm), h128n_in=_bf(-Hm),
            identb_in=_bf(np.eye(128)), identf_in=_f32c(np.eye(128)),
            mask_e=_bf(mask_e), mask_o=_bf(mask_o),
        )
        in_maps.append(m)
    return in_maps


def kernel(**inputs):
    nc = _get_program()
    in_maps = make_in_maps(inputs)
    res = run_bass_kernel_spmd(nc, in_maps, core_ids=list(range(8)))
    out = np.zeros((B, S, D), dtype=np.float32)
    for core in range(8):
        b, p = core // 2, core % 2
        o = res.results[core]["out"]
        out.reshape(B, NT, 128, D)[b, np.arange(p, NT, 2)] = o
    return out
